# version 09
# baseline (speedup 1.0000x reference)
import contextlib
import numpy as np
import ml_dtypes
import concourse.bass as bass
import concourse.mybir as mybir
from concourse.bass_utils import run_bass_kernel_spmd

F32 = mybir.dt.float32
BF16 = mybir.dt.bfloat16
I32 = mybir.dt.int32
AF = mybir.ActivationFunctionType
ALU = mybir.AluOpType
AX = mybir.AxisListType
BF = ml_dtypes.bfloat16

D_MODEL = 1024
D_INNER = 2048
D_STATE = 16
L = 2048
TC = 512
NT = L // TC
TF = 256
EPS = 1e-5
MAGIC = 12582912.0

NC_D = D_MODEL // 128
NC_I = D_INNER // 128
NC_H = 4096 // 128
BLK = 128 * 128


class Sched:
    ENGINES = ("sync", "tensor", "vector", "scalar", "gpsimd")

    def __init__(self):
        self.q = {e: [] for e in self.ENGINES}
        self.cnt = {}

    def emit(self, eng, fn, waits=(), inc=None, amt=1):
        ws = []
        for w in waits:
            if w is None or not w[1]:
                continue
            sem, val = w
            assert val <= self.cnt.get(sem, 0), (
                f"wait on future token {sem}={val} > {self.cnt.get(sem, 0)}")
            ws.append((sem, val))
        tok = None
        if inc is not None:
            self.cnt[inc] = self.cnt.get(inc, 0) + amt
            tok = self.cnt[inc]
        self.q[eng].append((fn, ws, inc, amt))
        return tok

    def realize(self, block, sems):
        own = {"tensor": "t", "vector": "v", "scalar": "s", "gpsimd": "g"}
        for eng in self.ENGINES:
            items = self.q[eng]
            own_sem = own.get(eng)

            def make(items=items, own_sem=own_sem):
                def section(e):
                    waited = {}
                    for fn, waits, inc, amt in items:
                        for sem_name, val in waits:
                            if sem_name == own_sem:
                                continue
                            if waited.get(sem_name, -1) >= val:
                                continue
                            e.wait_ge(sems[sem_name], val)
                            waited[sem_name] = val
                        inst = fn(e)
                        if inc is not None:
                            assert inst is not None
                            inst.then_inc(sems[inc], amt)
                return section
            getattr(block, eng)(make())


def build_nc(debug=0):
    nc = bass.Bass("TRN2", target_bir_lowering=False)

    def din(name, shape):
        return nc.dram_tensor(name, shape, F32, kind="ExternalInput")

    x_t = din("x", [D_MODEL, L])
    wxT_t = din("wxT", [D_MODEL, D_INNER])
    wzT_t = din("wzT", [D_MODEL, D_INNER])
    wxpT_t = din("wxpT", [D_INNER, 96])
    wdtT_t = din("wdtT", [64, D_INNER])
    woutT_t = din("woutT", [D_INNER, D_MODEL])
    wgT_t = din("wgT", [D_MODEL, 4096])
    wuT_t = din("wuT", [D_MODEL, 4096])
    wdT_t = din("wdT", [4096, D_MODEL])
    normw_t = din("normw", [D_MODEL])
    bx_t = din("bx", [D_INNER])
    bz_t = din("bz", [D_INNER])
    convw_t = din("convw", [D_INNER, 4])
    convb_t = din("convb", [D_INNER])
    bdt_t = din("bdt", [D_INNER])
    alog_t = din("alog", [D_INNER, D_STATE])
    dvec_t = din("dvec", [D_INNER])
    bout_t = din("bout", [D_MODEL])
    eyef_t = din("eyef", [128, 128])

    out_t = nc.dram_tensor("out", [D_MODEL, L], F32, kind="ExternalOutput")

    wxblk_t = nc.dram_tensor("wxblk", [NC_I * NC_D * BLK], BF16)
    wzblk_t = nc.dram_tensor("wzblk", [NC_I * NC_D * BLK], BF16)
    woblk_t = nc.dram_tensor("woblk", [NC_D * NC_I * BLK], BF16)
    wgblk_t = nc.dram_tensor("wgblk", [NC_H * NC_D * BLK], BF16)
    wublk_t = nc.dram_tensor("wublk", [NC_H * NC_D * BLK], BF16)
    wdblk_t = nc.dram_tensor("wdblk", [NC_D * NC_H * BLK], BF16)

    sch = Sched()
    SEMS = {}
    LU = {}

    with contextlib.ExitStack() as st:
        def sb(name, shape, dt=F32):
            return st.enter_context(nc.sbuf_tensor(name, shape, dt))

        def ps(name, shape):
            return st.enter_context(nc.psum_tensor(name, shape, F32))

        wxpT_bf = sb("wxpT_bf", [128, NC_I * 96], BF16)
        wdtT_bf = sb("wdtT_bf", [64, D_INNER], BF16)
        normw_c = sb("normw_c", [128, NC_D])
        bx_c = sb("bx_c", [128, NC_I])
        bz_c = sb("bz_c", [128, NC_I])
        convw_c = sb("convw_c", [128, NC_I * 4])
        convb_c = sb("convb_c", [128, NC_I])
        bdt_c = sb("bdt_c", [128, NC_I])
        acol = sb("acol", [128, NC_I * D_STATE])
        dvec_c = sb("dvec_c", [128, NC_I])
        bout_c = sb("bout_c", [128, NC_D])
        eye_f = sb("eye_f", [128, 128])
        eye_bf = sb("eye_bf", [128, 128], BF16)
        ddrot = sb("ddrot", [128, 256], BF16)
        ones_col_f = sb("ones_col_f", [128, 1])
        ones_col_bf = sb("ones_col_bf", [128, 1], BF16)
        ones_all_f = sb("ones_all_f", [128, 128])
        ones_all_bf = sb("ones_all_bf", [128, 128], BF16)
        hcarry = sb("hcarry", [128, 2 * NC_I * D_STATE + 8])
        halo = sb("halo", [128, NC_I * 3])
        corr_g = sb("corr_g", [128, NC_H])
        corr_u = sb("corr_u", [128, NC_H])
        corr_d = sb("corr_d", [128, NC_D])
        thcol = sb("thcol", [128, 6])
        meancol = sb("meancol", [128, 3])
        cvals = sb("cvals", [128, 8])

        G1 = sb("G1", [128, 8192])
        G3 = sb("G3", [128, 2048])
        G4 = sb("G4", [128, 2048])
        G5 = sb("G5", [128, 3328])
        G6 = sb("G6", [128, 2064])
        G7 = sb("G7", [128, 2048])
        rowsM = lambda off, n: G3[0:1, off:off + n]
        rowsF = lambda off, n: G6[0:1, off:off + n]
        rowbuf = G6
        RB = 1024
        absq = G7

        B1 = sb("B1", [128, 8192], BF16)
        B2 = sb("B2", [128, 8192], BF16)
        B3 = sb("B3", [128, 8192], BF16)
        B4 = sb("B4", [128, 8192], BF16)
        B5 = sb("B5", [128, 8192], BF16)
        B6 = sb("B6", [128, 4096], BF16)
        B7 = sb("B7", [128, 4096], BF16)
        wstage_bf = B7
        wi_rot = sb("wi_rot", [128, 512], I32)

        xt = sb("xt", [128, NC_D * TC])
        xd_bf = sb("xd_bf", [96, TC], BF16)

        pA = [ps("pA0", [128, 512]), ps("pA1", [128, 512])]
        pB = [ps("pB0", [128, 512]), ps("pB1", [128, 512])]
        pY = [ps("pY0", [128, 512]), ps("pY1", [128, 512])]
        pC = [ps("pC0", [128, 512]), ps("pC1", [128, 512])]

        for sname in ("dy", "dg", "t", "v", "s", "g"):
            SEMS[sname] = st.enter_context(nc.semaphore("sem_" + sname))

        em = sch.emit

        def dma(eng, sem, dst, src, waits=(), slow=False):
            return sch.emit(
                eng,
                lambda e, d=dst, s=src, sw=slow: e.dma_start(
                    d, s, allow_slow_non_contiguous=sw) if sw else
                e.dma_start(d, s),
                waits=waits, inc=sem, amt=16)

        def ap(t, off, dims):
            return bass.AP(t, off, dims)

        def sl(pool, off, n):
            return pool[:, off:off + n]

        bw = []

        def set_barrier():
            nonlocal bw
            bw = [(s, sch.cnt.get(s, 0)) for s in ("dy", "dg", "t", "v", "s", "g")]

        def W(*extra):
            return list(bw) + [w for w in extra if w is not None]

        def lu(key):
            return LU.get(key)

        tok_sm = None
        for dst, t, n in ((normw_c, normw_t, NC_D), (bx_c, bx_t, NC_I),
                          (bz_c, bz_t, NC_I), (convb_c, convb_t, NC_I),
                          (bdt_c, bdt_t, NC_I), (dvec_c, dvec_t, NC_I),
                          (bout_c, bout_t, NC_D)):
            tok_sm = dma("sync", "dy", dst[:, 0:n], ap(t, 0, [[1, 128], [128, n]]),
                         slow=True)
        tok_sm = dma("sync", "dy", convw_c[:, 0:NC_I * 4],
                     ap(convw_t, 0, [[4, 128], [512, NC_I], [1, 4]]))
        tok_sm = dma("sync", "dy", acol[:, 0:NC_I * D_STATE],
                     ap(alog_t, 0, [[16, 128], [2048, NC_I], [1, 16]]))
        tok_eye = dma("sync", "dy", eye_f[:], ap(eyef_t, 0, [[128, 128], [1, 128]]))

        CONSTS = (EPS, 1.0, -1.0, -0.5, 0.0, 0.5, 127.0, 2.0)
        for j, cv in enumerate(CONSTS):
            em("gpsimd", lambda e, j=j, cv=cv: e.memset(cvals[:, j:j + 1], cv))
            nc.const_aps.register(F32, cv, cvals[:, j:j + 1])
        em("gpsimd", lambda e: e.memset(ones_col_f[:], 1.0))
        em("gpsimd", lambda e: e.memset(ones_col_bf[:], 1.0))
        em("gpsimd", lambda e: e.memset(ones_all_f[:], 1.0))
        em("gpsimd", lambda e: e.memset(ones_all_bf[:], 1.0))
        em("gpsimd", lambda e: e.memset(hcarry[:], 0.0))
        em("gpsimd", lambda e: e.memset(halo[:], 0.0))
        em("gpsimd", lambda e: e.memset(corr_g[:, 0:NC_H], 0.0))
        em("gpsimd", lambda e: e.memset(corr_u[:, 0:NC_H], 0.0))
        tok_g_ms = em("gpsimd", lambda e: e.memset(corr_d[:, 0:NC_D], 0.0), inc="g")

        tok_eyebf = em("vector", lambda e: e.tensor_copy(eye_bf[:], eye_f[:]),
                       waits=[("dy", tok_eye)], inc="v")
        tok_A = em("scalar", lambda e: e.activation(acol[:], acol[:], AF.Exp),
                   waits=[("dy", tok_sm)], inc="s")
        tok_A = em("vector",
                   lambda e: e.tensor_scalar(acol[:], acol[:], -1.0, None, ALU.mult),
                   waits=[("s", tok_A)], inc="v")
        tok_v_cast = tok_A
        tok_g_wr = 0
        dg_before = sch.cnt.get("dg", 0)
        for (src_t, blk_t, nk, nb) in ((wxT_t, wxblk_t, NC_D, NC_I),
                                       (wzT_t, wzblk_t, NC_D, NC_I),
                                       (woutT_t, woblk_t, NC_I, NC_D)):
            ncols = nb * 128
            for k in range(nk):
                tok_ld = dma("sync", "dy", G3[:, 0:ncols],
                             ap(src_t, k * 128 * ncols, [[ncols, 128], [1, ncols]]),
                             waits=[("v", tok_v_cast)])
                tok_v_cast = em("vector",
                                lambda e, ncols=ncols: e.tensor_copy(
                                    wstage_bf[:, 0:ncols], G3[:, 0:ncols]),
                                waits=[("dy", tok_ld), ("dg", tok_g_wr)], inc="v")
                for b in range(nb):
                    tok_g_wr = dma("gpsimd", "dg",
                                   ap(blk_t, (b * nk + k) * BLK,
                                      [[128, 128], [1, 128]]),
                                   wstage_bf[:, b * 128:(b + 1) * 128],
                                   waits=[("v", tok_v_cast)])
        LU["blk_mw"] = ("dg", tok_g_wr)

        for k in range(NC_I):
            tok_ld = dma("sync", "dy", G3[:, 0:96],
                         ap(wxpT_t, k * 128 * 96, [[96, 128], [1, 96]]),
                         waits=[("v", tok_v_cast)])
            tok_v_cast = em("vector",
                            lambda e, k=k: e.tensor_copy(
                                sl(wxpT_bf, k * 96, 96), G3[:, 0:96]),
                            waits=[("dy", tok_ld)], inc="v")
        tok_ld = dma("sync", "dy", G3[0:64, 0:2048],
                     ap(wdtT_t, 0, [[2048, 64], [1, 2048]]),
                     waits=[("v", tok_v_cast)])
        tok_v_cast = em("vector",
                        lambda e: e.tensor_copy(wdtT_bf[:], G3[0:64, 0:2048]),
                        waits=[("dy", tok_ld)], inc="v")

        FST = [0, 4096]
        ffn_specs = ((wgT_t, NC_H, NC_D, 0), (wuT_t, NC_H, NC_D, 1),
                     (wdT_t, NC_D, NC_H, 2))
        ci = 0
        chunk_of = {0: [], 1: [], 2: []}
        tok_s_abs = None
        for (src_t, nb, nk, wi) in ffn_specs:
            ncols = nb * 128
            for k in range(nk):
                slot = FST[ci % 2]
                tok_ld = dma("sync", "dy", sl(G1, slot, ncols),
                             ap(src_t, k * 128 * ncols, [[ncols, 128], [1, ncols]]),
                             waits=[("s", tok_s_abs), ("v", tok_v_cast)])
                tok_s_abs = em("scalar",
                               lambda e, slot=slot, ncols=ncols, ci=ci: e.activation(
                                   sl(B2, 0, ncols), sl(G1, slot, ncols), AF.Abs,
                                   accum_out=absq[:, ci:ci + 1]),
                               waits=[("dy", tok_ld)], inc="s")
                chunk_of[wi].append(ci)
                ci += 1
        tok_v_red = None
        for (src_t, nb, nk, wi) in ffn_specs:
            c0, c1 = chunk_of[wi][0], chunk_of[wi][-1] + 1
            tok_v_red = em("vector",
                           lambda e, c0=c0, c1=c1, wi=wi: e.reduce_sum(
                               G7[:, 48 + wi:49 + wi], absq[:, c0:c1], axis=AX.X),
                           waits=[("s", tok_s_abs)], inc="v")
        tok_t_tot = em("tensor",
                       lambda e: e.matmul(pB[1][0:1, 0:3], ones_col_f[:],
                                          G7[:, 48:51], start=True, stop=True),
                       waits=[("v", tok_v_red), ("g", tok_g_ms)], inc="t")
        tok_s_tot = em("scalar",
                       lambda e: e.copy(G6[0:1, 16:19], pB[1][0:1, 0:3]),
                       waits=[("t", tok_t_tot)], inc="s")
        tok_t_mb = em("tensor",
                      lambda e: e.matmul(pB[0][:, 0:3], ones_all_f[0:1, :],
                                         G6[0:1, 16:19], start=True, stop=True),
                      waits=[("s", tok_s_tot)], inc="t")
        tok_s_mc = em("scalar", lambda e: e.copy(meancol[:, 0:3], pB[0][:, 0:3]),
                      waits=[("t", tok_t_mb)], inc="s")
        tok_v_th = None
        for (src_t, nb, nk, wi) in ffn_specs:
            nel = nb * 128 * nk * 128
            tok_v_th = em("vector",
                          lambda e, wi=wi, nel=nel: e.tensor_scalar(
                              meancol[:, wi:wi + 1], meancol[:, wi:wi + 1],
                              1.0 / nel, 1e-5, ALU.mult, ALU.max),
                          waits=[("s", tok_s_mc)], inc="v")
        for wi in range(3):
            tok_v_th = em("vector",
                          lambda e, wi=wi: e.tensor_scalar(
                              thcol[:, 2 * wi:2 * wi + 1], meancol[:, wi:wi + 1],
                              0.5, None, ALU.mult),
                          waits=[("v", tok_v_th)], inc="v")
            tok_v_th = em("vector",
                          lambda e, wi=wi: e.tensor_scalar(
                              thcol[:, 2 * wi + 1:2 * wi + 2],
                              meancol[:, wi:wi + 1], -0.5, None, ALU.mult),
                          inc="v")

        corr_acc = {0: corr_g, 1: corr_u, 2: corr_d}
        blk_dst = {0: wgblk_t, 1: wublk_t, 2: wdblk_t}
        WQ = [0, 4096]
        tok_g_wq = tok_g_wr
        tok_v_q = tok_v_th
        tok_t_corr = sch.cnt.get("t", 0)
        tok_v_corr = tok_v_th
        ci = 0
        for (src_t, nb, nk, wi) in ffn_specs:
            ncols = nb * 128
            for k in range(nk):
                slot = FST[ci % 2]
                wslot = WQ[ci % 2]
                tok_ld = dma("sync", "dy", sl(G1, slot, ncols),
                             ap(src_t, k * 128 * ncols, [[ncols, 128], [1, ncols]]),
                             waits=[("v", tok_v_q)])
                tok_v_q = em("vector",
                             lambda e, slot=slot, ncols=ncols, wi=wi:
                             e.tensor_single_scalar(
                                 sl(B1, 0, ncols), sl(G1, slot, ncols),
                                 thcol[:, 2 * wi:2 * wi + 1], ALU.is_ge),
                             waits=[("dy", tok_ld), ("v", tok_v_th)], inc="v")
                tok_v_q = em("vector",
                             lambda e, slot=slot, ncols=ncols, wi=wi:
                             e.tensor_single_scalar(
                                 sl(G1, slot, ncols), sl(G1, slot, ncols),
                                 thcol[:, 2 * wi + 1:2 * wi + 2], ALU.is_le),
                             inc="v")
                tok_v_q = em("vector",
                             lambda e, slot=slot, wslot=wslot, ncols=ncols:
                             e.tensor_sub(sl(B2, wslot, ncols), sl(B1, 0, ncols),
                                          sl(G1, slot, ncols)),
                             waits=[("dg", tok_g_wq)], inc="v")
                for b in range(nb):
                    tok_g_wq = dma("gpsimd", "dg",
                                   ap(blk_dst[wi], (b * nk + k) * BLK,
                                      [[128, 128], [1, 128]]),
                                   sl(B2, wslot + b * 128, 128),
                                   waits=[("v", tok_v_q)])
                for b in range(nb):
                    pj = b % 2
                    tok_t_corr = em("tensor",
                                    lambda e, wslot=wslot, b=b, pj=pj: e.matmul(
                                        pC[pj][:, 0:1],
                                        sl(B2, wslot + b * 128, 128),
                                        ones_col_bf[:], start=True, stop=True),
                                    waits=[("v", tok_v_q),
                                           LU.get(("corrv", pj))], inc="t")
                    tok_v_corr = em("vector",
                                    lambda e, wi=wi, b=b, pj=pj: e.tensor_add(
                                        corr_acc[wi][:, b:b + 1],
                                        corr_acc[wi][:, b:b + 1],
                                        pC[pj][:, 0:1]),
                                    waits=[("t", tok_t_corr)], inc="v")
                    LU[("corrv", pj)] = ("v", tok_v_corr)
                ci += 1
        tok_v_corr2 = tok_v_corr
        for wi, nb in ((0, NC_H), (1, NC_H), (2, NC_D)):
            tok_v_corr2 = em("vector",
                             lambda e, wi=wi, nb=nb: e.tensor_scalar(
                                 corr_acc[wi][:, 0:nb], corr_acc[wi][:, 0:nb],
                                 0.5, None, ALU.mult),
                             waits=[("v", tok_v_corr)], inc="v")
        LU["corr"] = ("v", tok_v_corr2)
        LU["blk_fw"] = ("dg", tok_g_wq)

        R_M, R_V, R_T1, R_R, R_RR = 0, 512, 1024, 1536, 1024
        BC_AL, BC_BE, BC_P, BC_Q, BC_X = 0, 512, 1024, 1536, 256
        SL_TL = [0, 512]
        SL_TL2 = [1024, 1536]
        SL_SQ = [1024, 1536]
        SL_SP1, SL_SP2 = 2048, 2560
        SL_XBR = [0, 515]
        SL_XC = [1032, 1548]
        SL_DA = [0, 512]
        SL_H = [1024, 1536]
        SL_DT = [0, 512]
        SL_DTX = [1024, 1536]
        SL_DBX = [2048, 2560]
        SL_PROD = [3072, 3584]
        SL_WX = [0, 1024]
        SL_WZ = [2048, 3072]
        SL_WO = [4096, 6144]
        SL_FT = [0, 512]
        SL_FU = [1024, 1536]
        SL_OB = [2048, 2560]
        SL_WG = [4096, 5120]
        SL_X2Q = 0
        SL_WD2 = [0, 4096]
        CQ = 3072

        def ln_stats(get_x, tf, waits_x, rg, RO):
            tok_t_sx = None
            for c in range(NC_D):
                par = c % 2
                tok_s_sq = em("scalar",
                              lambda e, c=c, par=par, tf=tf: e.activation(
                                  sl(G5, SL_SQ[par], tf), get_x(c), AF.Square),
                              waits=W(*waits_x, lu(("sq", par))), inc="s")
                LU[("sqv", c)] = ("s", tok_s_sq)
                tok_t_sx = em("tensor",
                              lambda e, c=c, tf=tf: e.matmul(
                                  pB[0][0:1, 0:tf], ones_col_f[:], get_x(c),
                                  start=(c == 0), stop=(c == NC_D - 1)),
                              waits=W(*waits_x, lu("pB0_s"), lu("pB0_v")),
                              inc="t")
                tok_t_sx = em("tensor",
                              lambda e, c=c, par=par, tf=tf: e.matmul(
                                  pB[1][0:1, 0:tf], ones_col_f[:],
                                  sl(G5, SL_SQ[par], tf),
                                  start=(c == 0), stop=(c == NC_D - 1)),
                              waits=W(lu(("sqv", c)), lu("pB1_s"), lu("pB1_v")),
                              inc="t")
                LU[("sq", par)] = ("t", tok_t_sx)
            tok = em("vector", lambda e, tf=tf: e.tensor_scalar(
                rg(RO["m"], tf), pB[0][0:1, 0:tf], 1.0 / D_MODEL,
                None, ALU.mult), waits=W(("t", tok_t_sx)), inc="v")
            tok = em("vector", lambda e, tf=tf: e.tensor_scalar(
                rg(RO["v"], tf), pB[1][0:1, 0:tf], 1.0 / D_MODEL,
                None, ALU.mult), inc="v")
            LU["pB0_v"] = ("v", tok)
            LU["pB1_v"] = ("v", tok)
            tok = em("vector", lambda e, tf=tf: e.tensor_mul(
                rg(RO["t1"], tf), rg(RO["m"], tf), rg(RO["m"], tf)), inc="v")
            tok_v_var = em("vector", lambda e, tf=tf: e.tensor_sub(
                rg(RO["v"], tf), rg(RO["v"], tf), rg(RO["t1"], tf)), inc="v")
            tok_s_r = em("scalar", lambda e, tf=tf: e.activation(
                rg(RO["t1"], tf), rg(RO["v"], tf), AF.Ln,
                bias=EPS), waits=W(("v", tok_v_var)), inc="s")
            tok_s_r = em("scalar", lambda e, tf=tf: e.activation(
                rg(RO["r"], tf), rg(RO["t1"], tf), AF.Exp,
                scale=-0.5), inc="s")
            return tok_s_r, tok_v_var

        def bcast_row(roff, boff, tf, vtok, par, rg):
            tok_t = em("tensor",
                       lambda e, roff=roff, par=par, tf=tf, rg=rg: e.matmul(
                           pB[par][:, 0:tf], ones_all_f[0:1, :],
                           rg(roff, tf), start=True, stop=True),
                       waits=W(("v", vtok), lu("pB%d_s" % par),
                               lu("pB%d_v" % par)), inc="t")
            tok_s = em("scalar",
                       lambda e, boff=boff, par=par, tf=tf: e.copy(
                           sl(G4, boff, tf), pB[par][:, 0:tf]),
                       waits=W(("t", tok_t)), inc="s")
            LU["pB%d_s" % par] = ("s", tok_s)
            return tok_s

        def tile_M(tt, debug=debug):
            base = tt * TC
            set_barrier()
            tok_x = None
            for c in range(NC_D):
                tok_x = dma("sync", "dy", sl(xt, c * TC, TC),
                            ap(x_t, c * 128 * L + base, [[L, 128], [1, TC]]),
                            waits=W())
            get_x = lambda c: sl(xt, c * TC, TC)
            RO = {"m": 0, "v": 512, "t1": 1024, "r": 1536}
            tok_s_r, tok_v_var = ln_stats(get_x, TC, [("dy", tok_x)], rowsM, RO)
            tok = em("vector", lambda e: e.tensor_mul(
                rowsM(1024, TC), rowsM(1536, TC), rowsM(1536, TC)),
                waits=W(("s", tok_s_r)), inc="v")
            tok_v_msq = em("vector", lambda e: e.tensor_mul(
                rowsM(1024, TC), rowsM(512, TC), rowsM(1024, TC)), inc="v")
            tok_s_rr = em("scalar", lambda e: e.activation(
                rowsM(1024, TC), rowsM(1024, TC), AF.Ln,
                bias=EPS), waits=W(("v", tok_v_msq)), inc="s")
            tok_s_rr = em("scalar", lambda e: e.activation(
                rowsM(1024, TC), rowsM(1024, TC), AF.Exp,
                scale=-0.5), inc="s")
            tok_v_al = em("vector", lambda e: e.tensor_mul(
                rowsM(512, TC), rowsM(1536, TC), rowsM(1024, TC)),
                waits=W(("s", tok_s_rr)), inc="v")
            tok_v_q_ = em("vector", lambda e: e.scalar_tensor_tensor(
                rowsM(1024, TC), rowsM(0, TC), -1.0,
                rowsM(1536, TC), ALU.mult, ALU.mult), inc="v")
            tok_v_be = em("vector", lambda e: e.scalar_tensor_tensor(
                rowsM(0, TC), rowsM(0, TC), -1.0,
                rowsM(512, TC), ALU.mult, ALU.mult), inc="v")
            tok_v_p = em("vector", lambda e: e.tensor_scalar(
                rowsM(1536, TC), rowsM(1536, TC), 1.0,
                None, ALU.add), inc="v")
            for i, (roff, boff) in enumerate(((512, BC_AL), (0, BC_BE),
                                              (1536, BC_P), (1024, BC_Q))):
                tok_s_bc = bcast_row(roff, boff, TC, tok_v_p, i % 2, rowsM)
            tok_v_r1 = None
            tok_s_xn = None
            for c in range(NC_D):
                par = c % 2
                tok = em("vector",
                         lambda e, c=c, par=par: e.tensor_mul(
                             sl(G5, SL_TL[par], TC), sl(xt, c * TC, TC),
                             sl(G4, BC_AL, TC)),
                         waits=W(("s", tok_s_bc), lu(("tl", par))), inc="v")
                tok_v_t2 = em("vector",
                              lambda e, par=par: e.tensor_add(
                                  sl(G5, SL_TL[par], TC), sl(G5, SL_TL[par], TC),
                                  sl(G4, BC_BE, TC)), inc="v")
                tok_s_xn = em("scalar",
                              lambda e, c=c, par=par: e.activation(
                                  sl(B6, c * TC, TC), sl(G5, SL_TL[par], TC),
                                  AF.Identity, scale=normw_c[:, c:c + 1]),
                              waits=W(("v", tok_v_t2)), inc="s")
                LU[("tl", par)] = ("s", tok_s_xn)
                tok = em("vector",
                         lambda e, c=c, par=par: e.tensor_mul(
                             sl(G5, SL_TL2[par], TC), sl(xt, c * TC, TC),
                             sl(G4, BC_P, TC)), inc="v")
                tok_v_r1 = em("vector",
                              lambda e, c=c, par=par: e.tensor_add(
                                  sl(xt, c * TC, TC), sl(G5, SL_TL2[par], TC),
                                  sl(G4, BC_Q, TC)), inc="v")
            tok_xn_all = tok_s_xn

            tok_s_xb = None
            tok_s_sz = None
            for b in range(NC_I):
                par = b % 2
                tok_wx = dma("sync", "dy", sl(B3, SL_WX[par], NC_D * 128),
                             ap(wxblk_t, b * NC_D * BLK,
                                [[128, 128], [BLK, NC_D], [1, 128]]),
                             waits=W(lu(("wx", par)), lu("blk_mw")))
                tok_wz = dma("sync", "dy", sl(B3, SL_WZ[par], NC_D * 128),
                             ap(wzblk_t, b * NC_D * BLK,
                                [[128, 128], [BLK, NC_D], [1, 128]]),
                             waits=W(lu(("wz", par))))
                tok_t_wx = None
                for k in range(NC_D):
                    tok_t_wx = em("tensor",
                                  lambda e, k=k, par=par: e.matmul(
                                      pA[par][:, 0:TC],
                                      sl(B3, SL_WX[par] + k * 128, 128),
                                      sl(B6, k * TC, TC),
                                      start=(k == 0), stop=(k == NC_D - 1)),
                                  waits=W(("dy", tok_wx), ("s", tok_xn_all),
                                          lu(("pAv", par)), lu(("pAs", par))),
                                  inc="t")
                LU[("wx", par)] = ("t", tok_t_wx)
                tok_s_xbr = em("scalar",
                               lambda e, b=b, par=par: e.activation(
                                   sl(G6, SL_XBR[par] + 3, TC), pA[par][:, 0:TC],
                                   AF.Identity, bias=bx_c[:, b:b + 1]),
                               waits=W(("t", tok_t_wx), lu(("xbr", par))),
                               inc="s")
                LU[("pAs", par)] = ("s", tok_s_xbr)
                tok_v_hi = em("vector",
                              lambda e, b=b, par=par: e.tensor_single_scalar(
                                  sl(G6, SL_XBR[par], 3), sl(halo, b * 3, 3),
                                  0.0, ALU.bypass),
                              waits=W(("s", tok_s_xbr)), inc="v")
                tok = em("vector",
                         lambda e, b=b, par=par: e.tensor_single_scalar(
                             sl(G6, SL_XC[par], TC), sl(G6, SL_XBR[par], TC),
                             convw_c[:, b * 4:b * 4 + 1], ALU.mult),
                         waits=W(lu(("xc", par))), inc="v")
                for kk in range(1, 4):
                    tok = em("vector",
                             lambda e, b=b, par=par, kk=kk: e.scalar_tensor_tensor(
                                 sl(G6, SL_XC[par], TC),
                                 sl(G6, SL_XBR[par] + kk, TC),
                                 convw_c[:, b * 4 + kk:b * 4 + kk + 1],
                                 sl(G6, SL_XC[par], TC), ALU.mult, ALU.add),
                             inc="v")
                tok_v_conv = tok
                tok_v_ho = em("vector",
                              lambda e, b=b, par=par: e.tensor_single_scalar(
                                  sl(halo, b * 3, 3), sl(G6, SL_XBR[par] + TC, 3),
                                  0.0, ALU.bypass),
                              inc="v")
                LU[("xbr", par)] = ("v", tok_v_ho)
                tok_s_xb = em("scalar",
                              lambda e, b=b, par=par: e.activation(
                                  sl(B4, b * TC, TC), sl(G6, SL_XC[par], TC),
                                  AF.Silu, bias=convb_c[:, b:b + 1]),
                              waits=W(("v", tok_v_conv), lu("B4free")), inc="s")
                LU[("xc", par)] = ("s", tok_s_xb)
                tok_t_wz = None
                for k in range(NC_D):
                    tok_t_wz = em("tensor",
                                  lambda e, k=k, par=par: e.matmul(
                                      pB[par][:, 0:TC],
                                      sl(B3, SL_WZ[par] + k * 128, 128),
                                      sl(B6, k * TC, TC),
                                      start=(k == 0), stop=(k == NC_D - 1)),
                                  waits=W(("dy", tok_wz), ("s", tok_xn_all),
                                          lu("pB%d_s" % par), lu("pB%d_v" % par)),
                                  inc="t")
                LU[("wz", par)] = ("t", tok_t_wz)
                tok_s_sz = em("scalar",
                              lambda e, b=b, par=par: e.activation(
                                  sl(B5, b * TC, TC), pB[par][:, 0:TC], AF.Silu,
                                  bias=bz_c[:, b:b + 1]),
                              waits=W(("t", tok_t_wz)), inc="s")
                LU["pB%d_s" % par] = ("s", tok_s_sz)
            tok_xb_all = tok_s_xb

            tok_t_xp = None
            for k in range(NC_I):
                tok_t_xp = em("tensor",
                              lambda e, k=k: e.matmul(
                                  pA[0][0:96, 0:TC], sl(wxpT_bf, k * 96, 96),
                                  sl(B4, k * TC, TC),
                                  start=(k == 0), stop=(k == NC_I - 1)),
                              waits=W(("s", tok_xb_all), lu(("pAv", 0)),
                                      lu(("pAs", 0))), inc="t")
            tok_s_xd = em("scalar",
                          lambda e: e.activation(xd_bf[:, 0:TC],
                                                 pA[0][0:96, 0:TC], AF.Copy),
                          waits=W(("t", tok_t_xp)), inc="s")
            LU[("pAs", 0)] = ("s", tok_s_xd)
            tok_bg = dma("gpsimd", "dg", B3[0:1, 0:D_STATE * TC],
                         xd_bf[64:64 + D_STATE, 0:TC],
                         waits=W(("s", tok_s_xd), lu(("wx", 0)), lu(("wx", 1)),
                                 lu(("wz", 0)), lu(("wz", 1))))
            tok_s_bcn = None
            tok_t_bl = []
            for n in range(D_STATE):
                par = n % 2
                tok_t_b = em("tensor",
                             lambda e, n=n, par=par: e.matmul(
                                 pB[par][:, 0:TC], ones_all_bf[0:1, :],
                                 B3[0:1, n * TC:(n + 1) * TC],
                                 start=True, stop=True),
                             waits=W(("dg", tok_bg), lu("pB%d_s" % par),
                                     lu("pB%d_v" % par), lu("B1free")),
                             inc="t")
                tok_t_bl.append(tok_t_b)
                tok_s_bcn = em("scalar",
                               lambda e, n=n, par=par: e.copy(
                                   sl(B1, n * TC, TC), pB[par][:, 0:TC]),
                               waits=W(("t", tok_t_b)), inc="s")
                LU["pB%d_s" % par] = ("s", tok_s_bcn)
            tok_cg = dma("gpsimd", "dg", B3[0:1, 0:D_STATE * TC],
                         xd_bf[80:80 + D_STATE, 0:TC],
                         waits=W(("t", tok_t_bl[-1])))
            for n in range(D_STATE):
                par = n % 2
                tok_t_c = em("tensor",
                             lambda e, n=n, par=par: e.matmul(
                                 pB[par][:, 0:TC], ones_all_bf[0:1, :],
                                 B3[0:1, n * TC:(n + 1) * TC],
                                 start=True, stop=True),
                             waits=W(("dg", tok_cg), ("s", tok_s_bcn)),
                             inc="t")
                tok_s_bcn = em("scalar",
                               lambda e, n=n, par=par: e.copy(
                                   sl(B2, n * TC, TC), pB[par][:, 0:TC]),
                               waits=W(("t", tok_t_c)), inc="s")
                LU["pB%d_s" % par] = ("s", tok_s_bcn)
                LU["bcmm_done"] = ("t", tok_t_c)

            tok_v_ysz = None
            for b in range(NC_I):
                par = b % 2
                tok_t_dt = em("tensor",
                              lambda e, b=b, par=par: e.matmul(
                                  pA[par][:, 0:TC],
                                  wdtT_bf[:, b * 128:(b + 1) * 128],
                                  xd_bf[0:64, 0:TC], start=True, stop=True),
                              waits=W(("s", tok_s_xd), lu(("pAv", par)),
                                      lu(("pAs", par))), inc="t")
                tok_s_sp = em("scalar",
                              lambda e, b=b, par=par: e.activation(
                                  sl(G5, SL_SP1, TC), pA[par][:, 0:TC], AF.Abs,
                                  bias=bdt_c[:, b:b + 1]),
                              waits=W(("t", tok_t_dt), lu("spv")), inc="s")
                tok_s_sp = em("scalar",
                              lambda e: e.activation(
                                  sl(G5, SL_SP2, TC), sl(G5, SL_SP1, TC),
                                  AF.Exp, scale=-1.0), inc="s")
                tok_s_sp = em("scalar",
                              lambda e: e.activation(
                                  sl(G5, SL_SP1, TC), sl(G5, SL_SP2, TC),
                                  AF.Ln, bias=1.0), inc="s")
                tok_v_dt = em("vector",
                              lambda e, b=b, par=par: e.tensor_scalar(
                                  sl(G5, SL_SP2, TC), pA[par][:, 0:TC],
                                  bdt_c[:, b:b + 1], 0.0, ALU.add, ALU.max),
                              waits=W(("s", tok_s_sp), lu(("dtS", par))),
                              inc="v")
                tok_v_dt = em("vector",
                              lambda e, par=par: e.tensor_add(
                                  sl(B7, SL_DT[par], TC), sl(G5, SL_SP2, TC),
                                  sl(G5, SL_SP1, TC)), inc="v")
                LU[("pAv", par)] = ("v", tok_v_dt)
                LU["spv"] = ("v", tok_v_dt)
                tok_v_dtx = em("vector",
                               lambda e, b=b, par=par: e.tensor_mul(
                                   sl(B7, SL_DTX[par], TC), sl(B7, SL_DT[par], TC),
                                   sl(B4, b * TC, TC)), inc="v")
                tok_v_dd = em("vector",
                              lambda e, b=b, par=par: e.tensor_single_scalar(
                                  sl(ddrot, par * 128, 128), eye_bf[:],
                                  dvec_c[:, b:b + 1], ALU.mult),
                              waits=W(lu(("ddT", par))), inc="v")
                tok_t_acc = None
                for n in range(D_STATE):
                    npar = n % 2
                    col = n * NC_I + b
                    prev = (b, n - 2) if n >= 2 else (b - 1, D_STATE - 2 + n)
                    tok_s_da = em("scalar",
                                  lambda e, b=b, n=n, par=par, npar=npar:
                                  e.activation(
                                      sl(G7, SL_DA[npar], TC),
                                      sl(B7, SL_DT[par], TC), AF.Exp,
                                      scale=acol[:, b * D_STATE + n:
                                                 b * D_STATE + n + 1]),
                                  waits=W(lu(("scanv", prev)),
                                          ("v", tok_v_dtx)), inc="s")
                    if n == D_STATE - 1:
                        LU[("dtS", par)] = ("s", tok_s_da)
                    tok_v_sc = em("vector",
                                  lambda e, n=n, par=par, npar=npar: e.tensor_mul(
                                      sl(B7, SL_DBX[npar], TC),
                                      sl(B7, SL_DTX[par], TC),
                                      sl(B1, n * TC, TC)),
                                  waits=W(lu(("acc", prev))), inc="v")
                    tok_v_sc = em("vector",
                                  lambda e, col=col, npar=npar:
                                  e.tensor_tensor_scan(
                                      sl(G7, SL_H[npar], TC),
                                      sl(G7, SL_DA[npar], TC),
                                      sl(B7, SL_DBX[npar], TC),
                                      hcarry[:, 2 * col + 1:2 * col + 2],
                                      ALU.mult, ALU.add),
                                  waits=W(("s", tok_s_da)), inc="v")
                    em("scalar",
                       lambda e, col=col, npar=npar: e.copy(
                           hcarry[:, 2 * col:2 * col + 2],
                           sl(G7, SL_H[npar] + TC - 2, 2)),
                       waits=W(("v", tok_v_sc)), inc="s")
                    if debug and tt == 0 and b == 15 and n == 0:
                        em("vector",
                           lambda e, npar=npar: e.tensor_copy(
                               G1[:, 0:TC], sl(G7, SL_H[npar], TC)), inc="v")
                    tok_v_sc = em("vector",
                                  lambda e, n=n, npar=npar: e.tensor_mul(
                                      sl(B7, SL_PROD[npar], TC),
                                      sl(G7, SL_H[npar], TC),
                                      sl(B2, n * TC, TC)), inc="v")
                    LU[("scanv", (b, n))] = ("v", tok_v_sc)
                    tok_t_acc = em("tensor",
                                   lambda e, b=b, n=n, npar=npar: e.matmul(
                                       pY[b % 2][:, 0:TC], eye_bf[:],
                                       sl(B7, SL_PROD[npar], TC),
                                       start=(n == 0), stop=False),
                                   waits=W(("v", tok_v_sc), lu(("ysz", b % 2))),
                                   inc="t")
                    LU[("acc", (b, n))] = ("t", tok_t_acc)
                tok_t_acc = em("tensor",
                               lambda e, b=b, par=par: e.matmul(
                                   pY[b % 2][:, 0:TC], sl(ddrot, par * 128, 128),
                                   sl(B4, b * TC, TC), start=False, stop=True),
                               waits=W(("v", tok_v_dd)), inc="t")
                LU[("ddT", par)] = ("t", tok_t_acc)
                tok_v_ysz = em("vector",
                               lambda e, b=b: e.tensor_mul(
                                   sl(B5, b * TC, TC), pY[b % 2][:, 0:TC],
                                   sl(B5, b * TC, TC)),
                               waits=W(("t", tok_t_acc)), inc="v")
                LU[("ysz", b % 2)] = ("v", tok_v_ysz)

            tok_v_x1 = None
            for m in range(NC_D):
                par = m % 2
                tok_wo = dma("sync", "dy", sl(B3, SL_WO[par], NC_I * 128),
                             ap(woblk_t, m * NC_I * BLK,
                                [[128, 128], [BLK, NC_I], [1, 128]]),
                             waits=W(lu(("wo", par)), lu("bcmm_done")))
                tok_t_wo = None
                for k in range(NC_I):
                    tok_t_wo = em("tensor",
                                  lambda e, k=k, par=par: e.matmul(
                                      pA[par][:, 0:TC],
                                      sl(B3, SL_WO[par] + k * 128, 128),
                                      sl(B5, k * TC, TC),
                                      start=(k == 0), stop=(k == NC_I - 1)),
                                  waits=W(("dy", tok_wo), ("v", tok_v_ysz),
                                          lu(("pAv", par)), lu(("pAs", par))),
                                  inc="t")
                LU[("wo", par)] = ("t", tok_t_wo)
                tok_v_x1 = em("vector",
                              lambda e, m=m, par=par: e.scalar_tensor_tensor(
                                  sl(xt, m * TC, TC), pA[par][:, 0:TC],
                                  bout_c[:, m:m + 1], sl(xt, m * TC, TC),
                                  ALU.add, ALU.add),
                              waits=W(("t", tok_t_wo)), inc="v")
                LU[("pAv", par)] = ("v", tok_v_x1)
            LU[("x1", tt)] = ("v", tok_v_x1)

        def quant_scales(tok_v_mx, tf, mode):
            nsl = tf // 128
            tok_s_row = None
            tok_v_mc = None
            for j in range(nsl):
                pj = j % 2
                tok_t_tr = em("tensor",
                              lambda e, j=j, pj=pj: e.matmul(
                                  pB[pj][:, 0:128],
                                  sl(G5, SL_FT[0] + j * 128, 128), eye_f[:],
                                  start=True, stop=True),
                              waits=W(("v", tok_v_mx), lu("pB%d_s" % pj),
                                      lu("pB%d_v" % pj)), inc="t")
                tok_v_mc = em("vector",
                              lambda e, j=j, pj=pj: e.reduce_max(
                                  sl(G5, CQ + 8 + j, 1),
                                  pB[pj][:, 0:128], axis=AX.X),
                              waits=W(("t", tok_t_tr)), inc="v")
                LU["pB%d_v" % pj] = ("v", tok_v_mc)
            for j in range(nsl):
                pj = j % 2
                tok_t_r = em("tensor",
                             lambda e, j=j, pj=pj: e.matmul(
                                 pB[pj][0:1, 0:128],
                                 sl(G5, CQ + 8 + j, 1), eye_f[:],
                                 start=True, stop=True),
                             waits=W(("v", tok_v_mc), lu("pB%d_s" % pj),
                                     lu("pB%d_v" % pj)), inc="t")
                tok_s_row = em("scalar",
                               lambda e, j=j, pj=pj: e.copy(
                                   G6[0:1, RB + j * 128:RB + (j + 1) * 128],
                                   pB[pj][0:1, 0:128]),
                               waits=W(("t", tok_t_r)), inc="s")
                LU["pB%d_s" % pj] = ("s", tok_s_row)
            tok_s_bc = None
            for j in range(nsl):
                pj = j % 2
                tok_t_b = em("tensor",
                             lambda e, j=j, pj=pj: e.matmul(
                                 pB[pj][:, 0:128], ones_all_f[0:1, :],
                                 G6[0:1, RB + j * 128:RB + (j + 1) * 128],
                                 start=True, stop=True),
                             waits=W(("s", tok_s_row), lu("pB%d_s" % pj),
                                     lu("pB%d_v" % pj)), inc="t")
                tok_s_bc = em("scalar",
                              lambda e, j=j, pj=pj: e.copy(
                                  sl(G4, BC_P + j * 128, 128),
                                  pB[pj][:, 0:128]),
                              waits=W(("t", tok_t_b)), inc="s")
                LU["pB%d_s" % pj] = ("s", tok_s_bc)
            tok_v_s = em("vector",
                         lambda e, tf=tf: e.tensor_scalar(
                             sl(G4, BC_P, tf), sl(G4, BC_P, tf), 1e-5, None,
                             ALU.max),
                         waits=W(("s", tok_s_bc)), inc="v")
            wis = (0, 1) if mode == 0 else (2,)
            dsts = (BC_Q, BC_X)
            for i, wi in enumerate(wis):
                tok_v_s = em("vector",
                             lambda e, tf=tf, wi=wi, i=i: e.scalar_tensor_tensor(
                                 sl(G4, dsts[i], tf), sl(G4, BC_P, tf),
                                 1.0 / 127.0,
                                 meancol[:, wi:wi + 1].broadcast_to((128, tf)),
                                 ALU.mult, ALU.mult), inc="v")
            tok_v_s = em("vector",
                         lambda e, tf=tf: e.reciprocal(
                             sl(G5, SL_FT[0], tf), sl(G4, BC_P, tf)), inc="v")
            tok_v_s = em("vector",
                         lambda e, tf=tf: e.tensor_scalar(
                             sl(G4, BC_P, tf), sl(G5, SL_FT[0], tf),
                             127.0, None, ALU.mult), inc="v")
            return tok_v_s

        def tile_F(ft, upto=None):
            base = ft * TF
            mtile = base // TC
            xoff = base % TC
            set_barrier()
            xs = lambda c: sl(xt, c * TC + xoff, TF)
            ROF = {"m": 0, "v": 256, "t1": 512, "r": 768}
            tok_s_r, _ = ln_stats(xs, TF, [lu(("x1", mtile))], rowsF, ROF)
            tok_v_be = em("vector", lambda e: e.scalar_tensor_tensor(
                rowsF(0, TF), rowsF(0, TF), -1.0,
                rowsF(768, TF), ALU.mult, ALU.mult),
                waits=W(("s", tok_s_r)), inc="v")
            for i, (roff, boff) in enumerate(((768, BC_AL), (0, BC_BE))):
                tok_s_bc = bcast_row(roff, boff, TF, tok_v_be, i % 2, rowsF)
            tok_v_x2 = None
            for c in range(NC_D):
                tok_v_x2 = em("vector",
                              lambda e, c=c: e.tensor_mul(
                                  sl(G3, c * TF, TF), xs(c), sl(G4, BC_AL, TF)),
                              waits=W(("s", tok_s_bc)), inc="v")
                tok_v_x2 = em("vector",
                              lambda e, c=c: e.tensor_add(
                                  sl(G3, c * TF, TF), sl(G3, c * TF, TF),
                                  sl(G4, BC_BE, TF)), inc="v")
            tok_v_mx = em("vector", lambda e: e.reduce_max(
                sl(G5, SL_FT[0], TF),
                G3[:, 0:NC_D * TF].rearrange("p (c t) -> p t c", t=TF),
                axis=AX.X, apply_absolute_value=True),
                waits=W(("v", tok_v_x2)), inc="v")
            tok_sq = quant_scales(tok_v_mx, TF, 0)
            tok_v_xq = None
            for c in range(NC_D):
                par = c % 2
                tok_v_xq = em("vector",
                              lambda e, c=c, par=par: e.tensor_mul(
                                  sl(G5, SL_FU[par], TF), sl(G3, c * TF, TF),
                                  sl(G4, BC_P, TF)),
                              waits=W(("v", tok_sq)), inc="v")
                tok_v_xq = em("vector",
                              lambda e, par=par: e.tensor_scalar(
                                  wi_rot[:, par * 256:par * 256 + TF].bitcast(F32),
                                  sl(G5, SL_FU[par], TF), MAGIC, None, ALU.add),
                              inc="v")
                tok_v_xq = em("vector",
                              lambda e, c=c, par=par: e.tensor_scalar(
                                  sl(B3, SL_X2Q + c * TF, TF),
                                  wi_rot[:, par * 256:par * 256 + TF].bitcast(F32),
                                  -MAGIC, -0.5, ALU.add, ALU.add), inc="v")
            if upto == "x2q":
                return
            tok_v_gu = None
            for j in range(NC_H):
                par = j % 2
                tok_wg = dma("sync", "dy", sl(B3, SL_WG[par], NC_D * 128),
                             ap(wgblk_t, j * NC_D * BLK,
                                [[128, 128], [BLK, NC_D], [1, 128]]),
                             waits=W(lu(("wgt", par)), lu("blk_fw")))
                tok_t_g = None
                for k in range(NC_D):
                    tok_t_g = em("tensor",
                                 lambda e, k=k, par=par: e.matmul(
                                     pA[par][:, 0:TF],
                                     sl(B3, SL_WG[par] + k * 128, 128),
                                     sl(B3, SL_X2Q + k * TF, TF),
                                     start=(k == 0), stop=(k == NC_D - 1)),
                                 waits=W(("dy", tok_wg), ("v", tok_v_xq),
                                         lu(("pAv", par)), lu(("pAs", par))),
                                 inc="t")
                LU[("wgt", par)] = ("t", tok_t_g)
                tok_v_go = em("vector",
                              lambda e, j=j, par=par: e.scalar_tensor_tensor(
                                  sl(G5, SL_FT[par], TF), pA[par][:, 0:TF],
                                  corr_g[:, j:j + 1], sl(G4, BC_Q, TF),
                                  ALU.add, ALU.mult),
                              waits=W(("t", tok_t_g), lu("corr"),
                                      lu(("ftS", par))), inc="v")
                LU[("pAv", par)] = ("v", tok_v_go)
                tok_s_gate = em("scalar",
                                lambda e, j=j, par=par: e.activation(
                                    sl(G1, j * TF, TF), sl(G5, SL_FT[par], TF),
                                    AF.Silu),
                                waits=W(("v", tok_v_go)), inc="s")
                LU[("ftS", par)] = ("s", tok_s_gate)
                tok_wu = dma("sync", "dy", sl(B3, SL_WG[par], NC_D * 128),
                             ap(wublk_t, j * NC_D * BLK,
                                [[128, 128], [BLK, NC_D], [1, 128]]),
                             waits=W(("t", tok_t_g)))
                tok_t_u = None
                for k in range(NC_D):
                    tok_t_u = em("tensor",
                                 lambda e, k=k, par=par: e.matmul(
                                     pY[par][:, 0:TF],
                                     sl(B3, SL_WG[par] + k * 128, 128),
                                     sl(B3, SL_X2Q + k * TF, TF),
                                     start=(k == 0), stop=(k == NC_D - 1)),
                                 waits=W(("dy", tok_wu), lu(("pYv", par))),
                                 inc="t")
                LU[("wgt", par)] = ("t", tok_t_u)
                tok_v_uo = em("vector",
                              lambda e, j=j, par=par: e.scalar_tensor_tensor(
                                  sl(G5, SL_FU[par], TF), pY[par][:, 0:TF],
                                  corr_u[:, j:j + 1], sl(G4, BC_X, TF),
                                  ALU.add, ALU.mult),
                              waits=W(("t", tok_t_u)), inc="v")
                LU[("pYv", par)] = ("v", tok_v_uo)
                tok_v_gu = em("vector",
                              lambda e, j=j, par=par: e.tensor_mul(
                                  sl(G1, j * TF, TF), sl(G5, SL_FU[par], TF),
                                  sl(G1, j * TF, TF)),
                              waits=W(("s", tok_s_gate)), inc="v")
            tok_v_mx = em("vector", lambda e: e.reduce_max(
                sl(G5, SL_FT[0], TF),
                G1[:, 0:NC_H * TF].rearrange("p (c t) -> p t c", t=TF),
                axis=AX.X, apply_absolute_value=True),
                waits=W(("v", tok_v_gu)), inc="v")
            tok_sq3 = quant_scales(tok_v_mx, TF, 1)
            tok_v_q3 = None
            for c in range(NC_H):
                par = c % 2
                tok_v_q3 = em("vector",
                              lambda e, c=c, par=par: e.tensor_mul(
                                  sl(G5, SL_FU[par], TF), sl(G1, c * TF, TF),
                                  sl(G4, BC_P, TF)),
                              waits=W(("v", tok_sq3), lu("B1free")), inc="v")
                tok_v_q3 = em("vector",
                              lambda e, par=par: e.tensor_scalar(
                                  wi_rot[:, par * 256:par * 256 + TF].bitcast(F32),
                                  sl(G5, SL_FU[par], TF), MAGIC, None, ALU.add),
                              inc="v")
                tok_v_q3 = em("vector",
                              lambda e, c=c, par=par: e.tensor_scalar(
                                  sl(B1, c * TF, TF),
                                  wi_rot[:, par * 256:par * 256 + TF].bitcast(F32),
                                  -MAGIC, -0.5, ALU.add, ALU.add), inc="v")
            tok_t_d = None
            for m in range(NC_D):
                par = m % 2
                tok_wd = dma("sync", "dy", sl(B4, SL_WD2[par], NC_H * 128),
                             ap(wdblk_t, m * NC_H * BLK,
                                [[128, 128], [BLK, NC_H], [1, 128]]),
                             waits=W(lu(("wdt", par))))
                tok_t_d = None
                for k in range(NC_H):
                    tok_t_d = em("tensor",
                                 lambda e, k=k, par=par: e.matmul(
                                     pA[par][:, 0:TF],
                                     sl(B4, SL_WD2[par] + k * 128, 128),
                                     sl(B1, k * TF, TF),
                                     start=(k == 0), stop=(k == NC_H - 1)),
                                 waits=W(("dy", tok_wd), ("v", tok_v_q3),
                                         lu(("pAv", par)), lu(("pAs", par))),
                                 inc="t")
                LU[("wdt", par)] = ("t", tok_t_d)
                tok_v_d = em("vector",
                             lambda e, m=m, par=par: e.scalar_tensor_tensor(
                                 sl(G5, SL_OB[par], TF), pA[par][:, 0:TF],
                                 corr_d[:, m:m + 1], sl(G4, BC_Q, TF),
                                 ALU.add, ALU.mult),
                             waits=W(("t", tok_t_d), lu(("outd", par))),
                             inc="v")
                LU[("pAv", par)] = ("v", tok_v_d)
                tok_v_d2 = em("vector",
                              lambda e, m=m, par=par: e.tensor_add(
                                  sl(G5, SL_OB[par], TF), sl(G5, SL_OB[par], TF),
                                  xs(m)), inc="v")
                tok_o = dma("gpsimd", "dg",
                            ap(out_t, m * 128 * L + base, [[L, 128], [1, TF]]),
                            sl(G5, SL_OB[par], TF),
                            waits=[("v", tok_v_d2)])
                LU[("outd", par)] = ("dg", tok_o)
            LU["B4free"] = ("t", tok_t_d)
            LU["B1free"] = ("t", tok_t_d)

        if debug == 3:
            d1 = nc.dram_tensor("dbg_absq", [128, 48], F32, kind="ExternalOutput")
            d2 = nc.dram_tensor("dbg_rows", [1, 80], F32, kind="ExternalOutput")
            d3 = nc.dram_tensor("dbg_mean", [128, 3], F32, kind="ExternalOutput")
            d4 = nc.dram_tensor("dbg_th", [128, 6], F32, kind="ExternalOutput")
            set_barrier()
            dma("gpsimd", "dg", bass.AP(d1, 0, [[48, 128], [1, 48]]), G7[:, 0:48], waits=W())
            em("scalar", lambda e: e.copy(G6[0:1, 16:64], pB[0][0:1, 0:48]),
               waits=W(), inc="s")
            set_barrier()
            dma("gpsimd", "dg", bass.AP(d2, 0, [[80, 1], [1, 80]]), G6[0:1, 0:80], waits=W())
            dma("gpsimd", "dg", bass.AP(d3, 0, [[3, 128], [1, 3]]), meancol[:, 0:3], waits=W())
            dma("gpsimd", "dg", bass.AP(d4, 0, [[6, 128], [1, 6]]), thcol[:, 0:6], waits=W())
        elif debug and debug != 5:
            if debug in (2, 4):
                dbg_specs = [
                    ("dbg_x2q", B3, BF16, 2048), ("dbg_gu", G1, F32, 8192),
                    ("dbg_guq", B1, BF16, 8192), ("dbg_g4", G4, F32, 2048),
                    ("dbg_corrg", corr_g, F32, 32), ("dbg_corrd", corr_d, F32, 8),
                    ("dbg_mean", meancol, F32, 3), ("dbg_x1", xt, F32, 4096),
                    ("dbg_cq", G5, F32, 3328), ("dbg_g6r", G6, F32, 2064),
                ]
            else:
                dbg_specs = [
                    ("dbg_xn", B6, BF16, 4096), ("dbg_xb", B4, BF16, 8192),
                    ("dbg_ysz", B5, BF16, 8192), ("dbg_x1", xt, F32, 4096),
                    ("dbg_hc", hcarry, F32, 512), ("dbg_B", B1, BF16, 8192),
                    ("dbg_C", B2, BF16, 8192), ("dbg_g4", G4, F32, 2048),
                    ("dbg_b7", B7, BF16, 4096), ("dbg_g7", G7, F32, 2048),
                    ("dbg_g1", G1, F32, 8192),
                ]
            dbg_ts = {}
            for nm, buf, dt_, n in dbg_specs:
                dbg_ts[nm] = nc.dram_tensor(nm, [128, n], dt_, kind="ExternalOutput")
            dbg_xd_t = nc.dram_tensor("dbg_xd", [96, 512], BF16, kind="ExternalOutput")
            tile_M(0)
            if debug == 1:
                set_barrier()
                em("vector", lambda e: e.tensor_copy(
                    G1[:, 512:1024], hcarry[:, 0:512]), waits=W(), inc="v")
            if debug == 2:
                tile_F(0)
            elif debug == 4:
                tile_F(0, upto="x2q")
            set_barrier()
            for nm, buf, dt_, n in dbg_specs:
                dma("gpsimd", "dg", bass.AP(dbg_ts[nm], 0, [[n, 128], [1, n]]),
                    buf[:, 0:n], waits=W())
            dma("gpsimd", "dg", bass.AP(dbg_xd_t, 0, [[512, 96], [1, 512]]),
                xd_bf[0:96, 0:512], waits=W())
        else:
            if debug == 5:
                x1d_ts = [nc.dram_tensor("dbg_x1t%d" % tt, [128, NC_D * TC],
                                         F32, kind="ExternalOutput")
                          for tt in range(NT)]
            for tt in range(NT):
                tile_M(tt)
                if debug == 5:
                    set_barrier()
                    dma("gpsimd", "dg",
                        bass.AP(x1d_ts[tt], 0,
                                [[NC_D * TC, 128], [1, NC_D * TC]]),
                        xt[:, 0:NC_D * TC], waits=W())
                for sub in range(TC // TF):
                    tile_F(tt * (TC // TF) + sub)

        em("gpsimd", lambda e: None, waits=[("dg", sch.cnt["dg"])])

        with nc.Block() as block:
            sch.realize(block, SEMS)

    return nc


_NC_CACHE = [None]


def kernel(**inputs):
    x = np.asarray(inputs["L0_attn_x"])
    t = lambda a: np.ascontiguousarray(np.asarray(a).T)
    base = {
        "wxT": t(inputs["L0_attn_Wx"]),
        "wzT": t(inputs["L0_attn_Wz"]),
        "wxpT": t(inputs["L0_attn_Wxp"]),
        "wdtT": t(inputs["L0_attn_Wdt"]),
        "woutT": t(inputs["L0_attn_Wout"]),
        "wgT": t(inputs["L0_ffn_w_gate"]),
        "wuT": t(inputs["L0_ffn_w_up"]),
        "wdT": t(inputs["L0_ffn_w_down"]),
        "normw": np.ascontiguousarray(inputs["L0_attn_norm_w"]),
        "bx": np.ascontiguousarray(inputs["L0_attn_bx"]),
        "bz": np.ascontiguousarray(inputs["L0_attn_bz"]),
        "convw": np.ascontiguousarray(
            np.asarray(inputs["L0_attn_conv_w"]).reshape(D_INNER, 4)),
        "convb": np.ascontiguousarray(inputs["L0_attn_conv_b"]),
        "bdt": np.ascontiguousarray(inputs["L0_attn_bdt"]),
        "alog": np.ascontiguousarray(inputs["L0_attn_A_log"]),
        "dvec": np.ascontiguousarray(inputs["L0_attn_D"]),
        "bout": np.ascontiguousarray(inputs["L0_attn_bout"]),
        "eyef": np.eye(128, dtype=np.float32),
    }
    in_maps = [dict(base, x=t(x[b])) for b in range(8)]
    if _NC_CACHE[0] is None:
        _NC_CACHE[0] = build_nc()
    res = run_bass_kernel_spmd(_NC_CACHE[0], in_maps, core_ids=list(range(8)))
    return np.ascontiguousarray(
        np.stack([res.results[b]["out"].T for b in range(8)])).astype(
            np.float32, copy=False)


# version 10
# speedup vs baseline: 1.0236x; 1.0236x over previous
import contextlib
import numpy as np
import ml_dtypes
import concourse.bass as bass
import concourse.mybir as mybir
from concourse.bass_utils import run_bass_kernel_spmd

F32 = mybir.dt.float32
BF16 = mybir.dt.bfloat16
I32 = mybir.dt.int32
AF = mybir.ActivationFunctionType
ALU = mybir.AluOpType
AX = mybir.AxisListType
BF = ml_dtypes.bfloat16

D_MODEL = 1024
D_INNER = 2048
D_STATE = 16
L = 2048
TC = 512
NT = L // TC
TF = 256
EPS = 1e-5
MAGIC = 12582912.0

NC_D = D_MODEL // 128
NC_I = D_INNER // 128
NC_H = 4096 // 128
BLK = 128 * 128


class Sched:
    ENGINES = ("sync", "tensor", "vector", "scalar", "gpsimd")

    def __init__(self):
        self.q = {e: [] for e in self.ENGINES}
        self.cnt = {}

    def emit(self, eng, fn, waits=(), inc=None, amt=1):
        ws = []
        for w in waits:
            if w is None or not w[1]:
                continue
            sem, val = w
            assert val <= self.cnt.get(sem, 0), (
                f"wait on future token {sem}={val} > {self.cnt.get(sem, 0)}")
            ws.append((sem, val))
        tok = None
        if inc is not None:
            self.cnt[inc] = self.cnt.get(inc, 0) + amt
            tok = self.cnt[inc]
        self.q[eng].append((fn, ws, inc, amt))
        return tok

    def realize(self, block, sems):
        own = {"tensor": "t", "vector": "v", "scalar": "s", "gpsimd": "g"}
        for eng in self.ENGINES:
            items = self.q[eng]
            own_sem = own.get(eng)

            def make(items=items, own_sem=own_sem):
                def section(e):
                    waited = {}
                    for fn, waits, inc, amt in items:
                        for sem_name, val in waits:
                            if sem_name == own_sem:
                                continue
                            if waited.get(sem_name, -1) >= val:
                                continue
                            e.wait_ge(sems[sem_name], val)
                            waited[sem_name] = val
                        inst = fn(e)
                        if inc is not None:
                            assert inst is not None
                            inst.then_inc(sems[inc], amt)
                return section
            getattr(block, eng)(make())


def build_nc(debug=0):
    nc = bass.Bass("TRN2", target_bir_lowering=False)

    def din(name, shape):
        return nc.dram_tensor(name, shape, F32, kind="ExternalInput")

    x_t = din("x", [D_MODEL, L])
    wxT_t = din("wxT", [D_MODEL, D_INNER])
    wzT_t = din("wzT", [D_MODEL, D_INNER])
    wxpT_t = din("wxpT", [D_INNER, 96])
    wdtT_t = din("wdtT", [64, D_INNER])
    woutT_t = din("woutT", [D_INNER, D_MODEL])
    wgT_t = din("wgT", [D_MODEL, 4096])
    wuT_t = din("wuT", [D_MODEL, 4096])
    wdT_t = din("wdT", [4096, D_MODEL])
    normw_t = din("normw", [D_MODEL])
    bx_t = din("bx", [D_INNER])
    bz_t = din("bz", [D_INNER])
    convw_t = din("convw", [D_INNER, 4])
    convb_t = din("convb", [D_INNER])
    bdt_t = din("bdt", [D_INNER])
    alog_t = din("alog", [D_INNER, D_STATE])
    dvec_t = din("dvec", [D_INNER])
    bout_t = din("bout", [D_MODEL])
    eyef_t = din("eyef", [128, 128])

    out_t = nc.dram_tensor("out", [D_MODEL, L], F32, kind="ExternalOutput")

    wxblk_t = nc.dram_tensor("wxblk", [NC_I * NC_D * BLK], BF16)
    wzblk_t = nc.dram_tensor("wzblk", [NC_I * NC_D * BLK], BF16)
    woblk_t = nc.dram_tensor("woblk", [NC_D * NC_I * BLK], BF16)
    wgblk_t = nc.dram_tensor("wgblk", [NC_H * NC_D * BLK], BF16)
    wublk_t = nc.dram_tensor("wublk", [NC_H * NC_D * BLK], BF16)
    wdblk_t = nc.dram_tensor("wdblk", [NC_D * NC_H * BLK], BF16)

    sch = Sched()
    SEMS = {}
    LU = {}

    with contextlib.ExitStack() as st:
        def sb(name, shape, dt=F32):
            return st.enter_context(nc.sbuf_tensor(name, shape, dt))

        def ps(name, shape):
            return st.enter_context(nc.psum_tensor(name, shape, F32))

        wxpT_bf = sb("wxpT_bf", [128, NC_I * 96], BF16)
        wdtT_bf = sb("wdtT_bf", [64, D_INNER], BF16)
        normw_c = sb("normw_c", [128, NC_D])
        bx_c = sb("bx_c", [128, NC_I])
        bz_c = sb("bz_c", [128, NC_I])
        convw_c = sb("convw_c", [128, NC_I * 4])
        convb_c = sb("convb_c", [128, NC_I])
        bdt_c = sb("bdt_c", [128, NC_I])
        acol = sb("acol", [128, NC_I * D_STATE])
        dvec_c = sb("dvec_c", [128, NC_I])
        bout_c = sb("bout_c", [128, NC_D])
        eye_f = sb("eye_f", [128, 128])
        eye_bf = sb("eye_bf", [128, 128], BF16)
        ddrot = sb("ddrot", [128, 256], BF16)
        ones_col_f = sb("ones_col_f", [128, 1])
        ones_col_bf = sb("ones_col_bf", [128, 1], BF16)
        ones_all_f = sb("ones_all_f", [128, 128])
        ones_all_bf = sb("ones_all_bf", [128, 128], BF16)
        hcarry = sb("hcarry", [128, 2 * NC_I * D_STATE + 8])
        halo = sb("halo", [128, NC_I * 3])
        corr_g = sb("corr_g", [128, NC_H])
        corr_u = sb("corr_u", [128, NC_H])
        corr_d = sb("corr_d", [128, NC_D])
        thcol = sb("thcol", [128, 6])
        meancol = sb("meancol", [128, 3])
        cvals = sb("cvals", [128, 8])

        G1 = sb("G1", [128, 8192])
        G3 = sb("G3", [128, 2048])
        G4 = sb("G4", [128, 2048])
        G5 = sb("G5", [128, 3328])
        G6 = sb("G6", [128, 2064])
        G7 = sb("G7", [128, 2048])
        rowsM = lambda off, n: G3[0:1, off:off + n]
        rowsF = lambda off, n: G6[0:1, off:off + n]
        rowbuf = G6
        RB = 1024
        absq = G7

        B1 = sb("B1", [128, 8192], BF16)
        B2 = sb("B2", [128, 8192], BF16)
        B3 = sb("B3", [128, 8192], BF16)
        B4 = sb("B4", [128, 8192], BF16)
        B5 = sb("B5", [128, 8192], BF16)
        B6 = sb("B6", [128, 4096], BF16)
        B7 = sb("B7", [128, 4096], BF16)
        wstage_bf = B7
        wi_rot = sb("wi_rot", [128, 512], I32)

        xt = sb("xt", [128, NC_D * TC])
        xd_bf = sb("xd_bf", [96, TC], BF16)

        pA = [ps("pA0", [128, 512]), ps("pA1", [128, 512])]
        pB = [ps("pB0", [128, 512]), ps("pB1", [128, 512])]
        pY = [ps("pY0", [128, 512]), ps("pY1", [128, 512])]
        pC = [ps("pC0", [128, 512]), ps("pC1", [128, 512])]

        for sname in ("dsm", "dst", "dgm", "dfa0", "dfa1", "dgw0", "dgw1",
                      "dx", "dwx0", "dwx1", "dwz0", "dwz1", "dwo0", "dwo1",
                      "dwg0", "dwg1", "dwd0", "dwd1", "dbc", "drb",
                      "dou0", "dou1", "dg", "t", "v", "s", "g"):
            SEMS[sname] = st.enter_context(nc.semaphore("sem_" + sname))

        em = sch.emit

        def dma(eng, sem, dst, src, waits=(), slow=False):
            return sch.emit(
                eng,
                lambda e, d=dst, s=src, sw=slow: e.dma_start(
                    d, s, allow_slow_non_contiguous=sw) if sw else
                e.dma_start(d, s),
                waits=waits, inc=sem, amt=16)

        def ap(t, off, dims):
            return bass.AP(t, off, dims)

        def sl(pool, off, n):
            return pool[:, off:off + n]

        bw = []

        BSEMS = ("dsm", "dst", "dgm", "dfa0", "dfa1", "dgw0", "dgw1", "dx",
                 "dwx0", "dwx1", "dwz0", "dwz1", "dwo0", "dwo1", "dwg0",
                 "dwg1", "dwd0", "dwd1", "dbc", "drb", "dou0", "dou1", "dg",
                 "t", "v", "s", "g")

        def set_barrier():
            nonlocal bw
            bw = [(s, sch.cnt.get(s, 0)) for s in BSEMS]

        def W(*extra):
            return list(bw) + [w for w in extra if w is not None]

        def lu(key):
            return LU.get(key)

        tok_sm = None
        for dst, t, n in ((normw_c, normw_t, NC_D), (bx_c, bx_t, NC_I),
                          (bz_c, bz_t, NC_I), (convb_c, convb_t, NC_I),
                          (bdt_c, bdt_t, NC_I), (dvec_c, dvec_t, NC_I),
                          (bout_c, bout_t, NC_D)):
            tok_sm = dma("sync", "dsm", dst[:, 0:n], ap(t, 0, [[1, 128], [128, n]]),
                         slow=True)
        tok_sm = dma("sync", "dsm", convw_c[:, 0:NC_I * 4],
                     ap(convw_t, 0, [[4, 128], [512, NC_I], [1, 4]]))
        tok_sm = dma("sync", "dsm", acol[:, 0:NC_I * D_STATE],
                     ap(alog_t, 0, [[16, 128], [2048, NC_I], [1, 16]]))
        tok_eye = dma("sync", "dsm", eye_f[:], ap(eyef_t, 0, [[128, 128], [1, 128]]))

        CONSTS = (EPS, 1.0, -1.0, -0.5, 0.0, 0.5, 127.0, 2.0)
        for j, cv in enumerate(CONSTS):
            em("gpsimd", lambda e, j=j, cv=cv: e.memset(cvals[:, j:j + 1], cv))
            nc.const_aps.register(F32, cv, cvals[:, j:j + 1])
        em("gpsimd", lambda e: e.memset(ones_col_f[:], 1.0))
        em("gpsimd", lambda e: e.memset(ones_col_bf[:], 1.0))
        em("gpsimd", lambda e: e.memset(ones_all_f[:], 1.0))
        em("gpsimd", lambda e: e.memset(ones_all_bf[:], 1.0))
        em("gpsimd", lambda e: e.memset(hcarry[:], 0.0))
        em("gpsimd", lambda e: e.memset(halo[:], 0.0))
        em("gpsimd", lambda e: e.memset(corr_g[:, 0:NC_H], 0.0))
        em("gpsimd", lambda e: e.memset(corr_u[:, 0:NC_H], 0.0))
        tok_g_ms = em("gpsimd", lambda e: e.memset(corr_d[:, 0:NC_D], 0.0), inc="g")

        tok_eyebf = em("vector", lambda e: e.tensor_copy(eye_bf[:], eye_f[:]),
                       waits=[("dsm", tok_eye)], inc="v")
        tok_A = em("scalar", lambda e: e.activation(acol[:], acol[:], AF.Exp),
                   waits=[("dsm", tok_sm)], inc="s")
        tok_A = em("vector",
                   lambda e: e.tensor_scalar(acol[:], acol[:], -1.0, None, ALU.mult),
                   waits=[("s", tok_A)], inc="v")
        tok_v_cast = tok_A
        tok_g_wr = 0
        dg_before = sch.cnt.get("dg", 0)
        for (src_t, blk_t, nk, nb) in ((wxT_t, wxblk_t, NC_D, NC_I),
                                       (wzT_t, wzblk_t, NC_D, NC_I),
                                       (woutT_t, woblk_t, NC_I, NC_D)):
            ncols = nb * 128
            for k in range(nk):
                tok_ld = dma("sync", "dst", G3[:, 0:ncols],
                             ap(src_t, k * 128 * ncols, [[ncols, 128], [1, ncols]]),
                             waits=[("v", tok_v_cast)])
                tok_v_cast = em("vector",
                                lambda e, ncols=ncols: e.tensor_copy(
                                    wstage_bf[:, 0:ncols], G3[:, 0:ncols]),
                                waits=[("dst", tok_ld), ("dgm", tok_g_wr)],
                                inc="v")
                for b in range(nb):
                    tok_g_wr = dma("gpsimd", "dgm",
                                   ap(blk_t, (b * nk + k) * BLK,
                                      [[128, 128], [1, 128]]),
                                   wstage_bf[:, b * 128:(b + 1) * 128],
                                   waits=[("v", tok_v_cast)])
        LU["blk_mw"] = ("dgm", tok_g_wr)

        for k in range(NC_I):
            tok_ld = dma("sync", "dst", G3[:, 0:96],
                         ap(wxpT_t, k * 128 * 96, [[96, 128], [1, 96]]),
                         waits=[("v", tok_v_cast)])
            tok_v_cast = em("vector",
                            lambda e, k=k: e.tensor_copy(
                                sl(wxpT_bf, k * 96, 96), G3[:, 0:96]),
                            waits=[("dst", tok_ld)], inc="v")
        tok_ld = dma("sync", "dst", G3[0:64, 0:2048],
                     ap(wdtT_t, 0, [[2048, 64], [1, 2048]]),
                     waits=[("v", tok_v_cast)])
        tok_v_cast = em("vector",
                        lambda e: e.tensor_copy(wdtT_bf[:], G3[0:64, 0:2048]),
                        waits=[("dst", tok_ld)], inc="v")

        FST = [0, 4096]
        ffn_specs = ((wgT_t, NC_H, NC_D, 0), (wuT_t, NC_H, NC_D, 1),
                     (wdT_t, NC_D, NC_H, 2))
        ci = 0
        chunk_of = {0: [], 1: [], 2: []}
        tok_s_abs = None
        for (src_t, nb, nk, wi) in ffn_specs:
            ncols = nb * 128
            for k in range(nk):
                slot = FST[ci % 2]
                sem = "dfa%d" % (ci % 2)
                tok_ld = dma("sync", sem, sl(G1, slot, ncols),
                             ap(src_t, k * 128 * ncols, [[ncols, 128], [1, ncols]]),
                             waits=[("s", tok_s_abs), ("v", tok_v_cast)])
                tok_s_abs = em("scalar",
                               lambda e, slot=slot, ncols=ncols, ci=ci: e.activation(
                                   sl(B2, 0, ncols), sl(G1, slot, ncols), AF.Abs,
                                   accum_out=absq[:, ci:ci + 1]),
                               waits=[(sem, tok_ld)], inc="s")
                chunk_of[wi].append(ci)
                ci += 1
        tok_v_red = None
        for (src_t, nb, nk, wi) in ffn_specs:
            c0, c1 = chunk_of[wi][0], chunk_of[wi][-1] + 1
            tok_v_red = em("vector",
                           lambda e, c0=c0, c1=c1, wi=wi: e.reduce_sum(
                               G7[:, 48 + wi:49 + wi], absq[:, c0:c1], axis=AX.X),
                           waits=[("s", tok_s_abs)], inc="v")
        tok_t_tot = em("tensor",
                       lambda e: e.matmul(pB[1][0:1, 0:3], ones_col_f[:],
                                          G7[:, 48:51], start=True, stop=True),
                       waits=[("v", tok_v_red), ("g", tok_g_ms)], inc="t")
        tok_s_tot = em("scalar",
                       lambda e: e.copy(G6[0:1, 16:19], pB[1][0:1, 0:3]),
                       waits=[("t", tok_t_tot)], inc="s")
        tok_t_mb = em("tensor",
                      lambda e: e.matmul(pB[0][:, 0:3], ones_all_f[0:1, :],
                                         G6[0:1, 16:19], start=True, stop=True),
                      waits=[("s", tok_s_tot)], inc="t")
        tok_s_mc = em("scalar", lambda e: e.copy(meancol[:, 0:3], pB[0][:, 0:3]),
                      waits=[("t", tok_t_mb)], inc="s")
        tok_v_th = None
        for (src_t, nb, nk, wi) in ffn_specs:
            nel = nb * 128 * nk * 128
            tok_v_th = em("vector",
                          lambda e, wi=wi, nel=nel: e.tensor_scalar(
                              meancol[:, wi:wi + 1], meancol[:, wi:wi + 1],
                              1.0 / nel, 1e-5, ALU.mult, ALU.max),
                          waits=[("s", tok_s_mc)], inc="v")
        for wi in range(3):
            tok_v_th = em("vector",
                          lambda e, wi=wi: e.tensor_scalar(
                              thcol[:, 2 * wi:2 * wi + 1], meancol[:, wi:wi + 1],
                              0.5, None, ALU.mult),
                          waits=[("v", tok_v_th)], inc="v")
            tok_v_th = em("vector",
                          lambda e, wi=wi: e.tensor_scalar(
                              thcol[:, 2 * wi + 1:2 * wi + 2],
                              meancol[:, wi:wi + 1], -0.5, None, ALU.mult),
                          inc="v")

        corr_acc = {0: corr_g, 1: corr_u, 2: corr_d}
        blk_dst = {0: wgblk_t, 1: wublk_t, 2: wdblk_t}
        WQ = [0, 4096]
        LU_WQ = {}
        tok_g_wq = 0
        tok_v_q = tok_v_th
        tok_t_corr = sch.cnt.get("t", 0)
        tok_v_corr = tok_v_th
        ci = 0
        for (src_t, nb, nk, wi) in ffn_specs:
            ncols = nb * 128
            for k in range(nk):
                slot = FST[ci % 2]
                wslot = WQ[ci % 2]
                sem = "dfa%d" % (ci % 2)
                wsem = "dgw%d" % (ci % 2)
                tok_ld = dma("sync", sem, sl(G1, slot, ncols),
                             ap(src_t, k * 128 * ncols, [[ncols, 128], [1, ncols]]),
                             waits=[("v", tok_v_q)])
                tok_v_q = em("vector",
                             lambda e, slot=slot, ncols=ncols, wi=wi:
                             e.tensor_single_scalar(
                                 sl(B1, 0, ncols), sl(G1, slot, ncols),
                                 thcol[:, 2 * wi:2 * wi + 1], ALU.is_ge),
                             waits=[(sem, tok_ld), ("v", tok_v_th)], inc="v")
                tok_v_q = em("vector",
                             lambda e, slot=slot, ncols=ncols, wi=wi:
                             e.tensor_single_scalar(
                                 sl(G1, slot, ncols), sl(G1, slot, ncols),
                                 thcol[:, 2 * wi + 1:2 * wi + 2], ALU.is_le),
                             inc="v")
                tok_v_q = em("vector",
                             lambda e, slot=slot, wslot=wslot, ncols=ncols:
                             e.tensor_sub(sl(B2, wslot, ncols), sl(B1, 0, ncols),
                                          sl(G1, slot, ncols)),
                             waits=[(wsem, LU_WQ.get(wsem, 0))], inc="v")
                for b in range(nb):
                    tok_g_wq = dma("gpsimd", wsem,
                                   ap(blk_dst[wi], (b * nk + k) * BLK,
                                      [[128, 128], [1, 128]]),
                                   sl(B2, wslot + b * 128, 128),
                                   waits=[("v", tok_v_q)])
                LU_WQ[wsem] = tok_g_wq
                for b in range(nb):
                    pj = b % 2
                    tok_t_corr = em("tensor",
                                    lambda e, wslot=wslot, b=b, pj=pj: e.matmul(
                                        pC[pj][:, 0:1],
                                        sl(B2, wslot + b * 128, 128),
                                        ones_col_bf[:], start=True, stop=True),
                                    waits=[("v", tok_v_q),
                                           LU.get(("corrv", pj))], inc="t")
                    tok_v_corr = em("vector",
                                    lambda e, wi=wi, b=b, pj=pj: e.tensor_add(
                                        corr_acc[wi][:, b:b + 1],
                                        corr_acc[wi][:, b:b + 1],
                                        pC[pj][:, 0:1]),
                                    waits=[("t", tok_t_corr)], inc="v")
                    LU[("corrv", pj)] = ("v", tok_v_corr)
                ci += 1
        tok_v_corr2 = tok_v_corr
        for wi, nb in ((0, NC_H), (1, NC_H), (2, NC_D)):
            tok_v_corr2 = em("vector",
                             lambda e, wi=wi, nb=nb: e.tensor_scalar(
                                 corr_acc[wi][:, 0:nb], corr_acc[wi][:, 0:nb],
                                 0.5, None, ALU.mult),
                             waits=[("v", tok_v_corr)], inc="v")
        LU["corr"] = ("v", tok_v_corr2)
        LU["blk_fw0"] = ("dgw0", LU_WQ.get("dgw0", 0))
        LU["blk_fw1"] = ("dgw1", LU_WQ.get("dgw1", 0))

        R_M, R_V, R_T1, R_R, R_RR = 0, 512, 1024, 1536, 1024
        BC_AL, BC_BE, BC_P, BC_Q, BC_X = 0, 512, 1024, 1536, 256
        SL_TL = [0, 512]
        SL_TL2 = [1024, 1536]
        SL_SQ = [1024, 1536]
        SL_SP1, SL_SP2 = 2048, 2560
        SL_XBR = [0, 515]
        SL_XC = [1032, 1548]
        SL_DA = [0, 512]
        SL_H = [1024, 1536]
        SL_DT = [0, 512]
        SL_DTX = [1024, 1536]
        SL_DBX = [2048, 2560]
        SL_PROD = [3072, 3584]
        SL_WX = [0, 1024]
        SL_WZ = [2048, 3072]
        SL_WO = [4096, 6144]
        SL_FT = [0, 512]
        SL_FU = [1024, 1536]
        SL_OB = [2048, 2560]
        SL_WG = [4096, 5120]
        SL_X2Q = 0
        SL_WD2 = [0, 4096]
        CQ = 3072

        def ln_stats(get_x, tf, waits_x, rg, RO):
            tok_t_sx = None
            for c in range(NC_D):
                par = c % 2
                tok_s_sq = em("scalar",
                              lambda e, c=c, par=par, tf=tf: e.activation(
                                  sl(G5, SL_SQ[par], tf), get_x(c), AF.Square),
                              waits=W(*waits_x, lu(("sq", par))), inc="s")
                LU[("sqv", c)] = ("s", tok_s_sq)
                tok_t_sx = em("tensor",
                              lambda e, c=c, tf=tf: e.matmul(
                                  pB[0][0:1, 0:tf], ones_col_f[:], get_x(c),
                                  start=(c == 0), stop=(c == NC_D - 1)),
                              waits=W(*waits_x, lu("pB0_s"), lu("pB0_v")),
                              inc="t")
                tok_t_sx = em("tensor",
                              lambda e, c=c, par=par, tf=tf: e.matmul(
                                  pB[1][0:1, 0:tf], ones_col_f[:],
                                  sl(G5, SL_SQ[par], tf),
                                  start=(c == 0), stop=(c == NC_D - 1)),
                              waits=W(lu(("sqv", c)), lu("pB1_s"), lu("pB1_v")),
                              inc="t")
                LU[("sq", par)] = ("t", tok_t_sx)
            tok = em("vector", lambda e, tf=tf: e.tensor_scalar(
                rg(RO["m"], tf), pB[0][0:1, 0:tf], 1.0 / D_MODEL,
                None, ALU.mult), waits=W(("t", tok_t_sx)), inc="v")
            tok = em("vector", lambda e, tf=tf: e.tensor_scalar(
                rg(RO["v"], tf), pB[1][0:1, 0:tf], 1.0 / D_MODEL,
                None, ALU.mult), inc="v")
            LU["pB0_v"] = ("v", tok)
            LU["pB1_v"] = ("v", tok)
            tok = em("vector", lambda e, tf=tf: e.tensor_mul(
                rg(RO["t1"], tf), rg(RO["m"], tf), rg(RO["m"], tf)), inc="v")
            tok_v_var = em("vector", lambda e, tf=tf: e.tensor_sub(
                rg(RO["v"], tf), rg(RO["v"], tf), rg(RO["t1"], tf)), inc="v")
            tok_s_r = em("scalar", lambda e, tf=tf: e.activation(
                rg(RO["t1"], tf), rg(RO["v"], tf), AF.Ln,
                bias=EPS), waits=W(("v", tok_v_var)), inc="s")
            tok_s_r = em("scalar", lambda e, tf=tf: e.activation(
                rg(RO["r"], tf), rg(RO["t1"], tf), AF.Exp,
                scale=-0.5), inc="s")
            return tok_s_r, tok_v_var

        def bcast_row(roff, boff, tf, vtok, par, rg):
            tok_t = em("tensor",
                       lambda e, roff=roff, par=par, tf=tf, rg=rg: e.matmul(
                           pB[par][:, 0:tf], ones_all_f[0:1, :],
                           rg(roff, tf), start=True, stop=True),
                       waits=W(("v", vtok), lu("pB%d_s" % par),
                               lu("pB%d_v" % par)), inc="t")
            tok_s = em("scalar",
                       lambda e, boff=boff, par=par, tf=tf: e.copy(
                           sl(G4, boff, tf), pB[par][:, 0:tf]),
                       waits=W(("t", tok_t)), inc="s")
            LU["pB%d_s" % par] = ("s", tok_s)
            return tok_s

        def tile_M(tt, debug=debug):
            base = tt * TC
            set_barrier()
            tok_x = None
            for c in range(NC_D):
                tok_x = dma("sync", "dx", sl(xt, c * TC, TC),
                            ap(x_t, c * 128 * L + base, [[L, 128], [1, TC]]),
                            waits=W())
            get_x = lambda c: sl(xt, c * TC, TC)
            RO = {"m": 0, "v": 512, "t1": 1024, "r": 1536}
            tok_s_r, tok_v_var = ln_stats(get_x, TC, [("dx", tok_x)], rowsM, RO)
            tok = em("vector", lambda e: e.tensor_mul(
                rowsM(1024, TC), rowsM(1536, TC), rowsM(1536, TC)),
                waits=W(("s", tok_s_r)), inc="v")
            tok_v_msq = em("vector", lambda e: e.tensor_mul(
                rowsM(1024, TC), rowsM(512, TC), rowsM(1024, TC)), inc="v")
            tok_s_rr = em("scalar", lambda e: e.activation(
                rowsM(1024, TC), rowsM(1024, TC), AF.Ln,
                bias=EPS), waits=W(("v", tok_v_msq)), inc="s")
            tok_s_rr = em("scalar", lambda e: e.activation(
                rowsM(1024, TC), rowsM(1024, TC), AF.Exp,
                scale=-0.5), inc="s")
            tok_v_al = em("vector", lambda e: e.tensor_mul(
                rowsM(512, TC), rowsM(1536, TC), rowsM(1024, TC)),
                waits=W(("s", tok_s_rr)), inc="v")
            tok_v_q_ = em("vector", lambda e: e.scalar_tensor_tensor(
                rowsM(1024, TC), rowsM(0, TC), -1.0,
                rowsM(1536, TC), ALU.mult, ALU.mult), inc="v")
            tok_v_be = em("vector", lambda e: e.scalar_tensor_tensor(
                rowsM(0, TC), rowsM(0, TC), -1.0,
                rowsM(512, TC), ALU.mult, ALU.mult), inc="v")
            tok_v_p = em("vector", lambda e: e.tensor_scalar(
                rowsM(1536, TC), rowsM(1536, TC), 1.0,
                None, ALU.add), inc="v")
            for i, (roff, boff) in enumerate(((512, BC_AL), (0, BC_BE),
                                              (1536, BC_P), (1024, BC_Q))):
                tok_s_bc = bcast_row(roff, boff, TC, tok_v_p, i % 2, rowsM)
            tok_v_r1 = None
            tok_s_xn = None
            for c in range(NC_D):
                par = c % 2
                tok = em("vector",
                         lambda e, c=c, par=par: e.tensor_mul(
                             sl(G5, SL_TL[par], TC), sl(xt, c * TC, TC),
                             sl(G4, BC_AL, TC)),
                         waits=W(("s", tok_s_bc), lu(("tl", par))), inc="v")
                tok_v_t2 = em("vector",
                              lambda e, par=par: e.tensor_add(
                                  sl(G5, SL_TL[par], TC), sl(G5, SL_TL[par], TC),
                                  sl(G4, BC_BE, TC)), inc="v")
                tok_s_xn = em("scalar",
                              lambda e, c=c, par=par: e.activation(
                                  sl(B6, c * TC, TC), sl(G5, SL_TL[par], TC),
                                  AF.Identity, scale=normw_c[:, c:c + 1]),
                              waits=W(("v", tok_v_t2)), inc="s")
                LU[("tl", par)] = ("s", tok_s_xn)
                tok = em("vector",
                         lambda e, c=c, par=par: e.tensor_mul(
                             sl(G5, SL_TL2[par], TC), sl(xt, c * TC, TC),
                             sl(G4, BC_P, TC)), inc="v")
                tok_v_r1 = em("vector",
                              lambda e, c=c, par=par: e.tensor_add(
                                  sl(xt, c * TC, TC), sl(G5, SL_TL2[par], TC),
                                  sl(G4, BC_Q, TC)), inc="v")
            tok_xn_all = tok_s_xn

            tok_s_xb = None
            tok_s_sz = None
            for b in range(NC_I):
                par = b % 2
                tok_wx = dma("sync", "dwx%d" % par, sl(B3, SL_WX[par], NC_D * 128),
                             ap(wxblk_t, b * NC_D * BLK,
                                [[128, 128], [BLK, NC_D], [1, 128]]),
                             waits=W(lu(("wx", par)), lu("blk_mw")))
                tok_wz = dma("sync", "dwz%d" % par, sl(B3, SL_WZ[par], NC_D * 128),
                             ap(wzblk_t, b * NC_D * BLK,
                                [[128, 128], [BLK, NC_D], [1, 128]]),
                             waits=W(lu(("wz", par))))
                tok_t_wx = None
                for k in range(NC_D):
                    tok_t_wx = em("tensor",
                                  lambda e, k=k, par=par: e.matmul(
                                      pA[par][:, 0:TC],
                                      sl(B3, SL_WX[par] + k * 128, 128),
                                      sl(B6, k * TC, TC),
                                      start=(k == 0), stop=(k == NC_D - 1)),
                                  waits=W(("dwx%d" % par, tok_wx), ("s", tok_xn_all),
                                          lu(("pAv", par)), lu(("pAs", par))),
                                  inc="t")
                LU[("wx", par)] = ("t", tok_t_wx)
                tok_s_xbr = em("scalar",
                               lambda e, b=b, par=par: e.activation(
                                   sl(G6, SL_XBR[par] + 3, TC), pA[par][:, 0:TC],
                                   AF.Identity, bias=bx_c[:, b:b + 1]),
                               waits=W(("t", tok_t_wx), lu(("xbr", par))),
                               inc="s")
                LU[("pAs", par)] = ("s", tok_s_xbr)
                tok_v_hi = em("vector",
                              lambda e, b=b, par=par: e.tensor_single_scalar(
                                  sl(G6, SL_XBR[par], 3), sl(halo, b * 3, 3),
                                  0.0, ALU.bypass),
                              waits=W(("s", tok_s_xbr)), inc="v")
                tok = em("vector",
                         lambda e, b=b, par=par: e.tensor_single_scalar(
                             sl(G6, SL_XC[par], TC), sl(G6, SL_XBR[par], TC),
                             convw_c[:, b * 4:b * 4 + 1], ALU.mult),
                         waits=W(lu(("xc", par))), inc="v")
                for kk in range(1, 4):
                    tok = em("vector",
                             lambda e, b=b, par=par, kk=kk: e.scalar_tensor_tensor(
                                 sl(G6, SL_XC[par], TC),
                                 sl(G6, SL_XBR[par] + kk, TC),
                                 convw_c[:, b * 4 + kk:b * 4 + kk + 1],
                                 sl(G6, SL_XC[par], TC), ALU.mult, ALU.add),
                             inc="v")
                tok_v_conv = tok
                tok_v_ho = em("vector",
                              lambda e, b=b, par=par: e.tensor_single_scalar(
                                  sl(halo, b * 3, 3), sl(G6, SL_XBR[par] + TC, 3),
                                  0.0, ALU.bypass),
                              inc="v")
                LU[("xbr", par)] = ("v", tok_v_ho)
                tok_s_xb = em("scalar",
                              lambda e, b=b, par=par: e.activation(
                                  sl(B4, b * TC, TC), sl(G6, SL_XC[par], TC),
                                  AF.Silu, bias=convb_c[:, b:b + 1]),
                              waits=W(("v", tok_v_conv), lu("B4free")), inc="s")
                LU[("xc", par)] = ("s", tok_s_xb)
                tok_t_wz = None
                for k in range(NC_D):
                    tok_t_wz = em("tensor",
                                  lambda e, k=k, par=par: e.matmul(
                                      pB[par][:, 0:TC],
                                      sl(B3, SL_WZ[par] + k * 128, 128),
                                      sl(B6, k * TC, TC),
                                      start=(k == 0), stop=(k == NC_D - 1)),
                                  waits=W(("dwz%d" % par, tok_wz),
                                          ("s", tok_xn_all),
                                          lu("pB%d_s" % par), lu("pB%d_v" % par)),
                                  inc="t")
                LU[("wz", par)] = ("t", tok_t_wz)
                tok_s_sz = em("scalar",
                              lambda e, b=b, par=par: e.activation(
                                  sl(B5, b * TC, TC), pB[par][:, 0:TC], AF.Silu,
                                  bias=bz_c[:, b:b + 1]),
                              waits=W(("t", tok_t_wz)), inc="s")
                LU["pB%d_s" % par] = ("s", tok_s_sz)
            tok_xb_all = tok_s_xb

            tok_t_xp = None
            for k in range(NC_I):
                tok_t_xp = em("tensor",
                              lambda e, k=k: e.matmul(
                                  pA[0][0:96, 0:TC], sl(wxpT_bf, k * 96, 96),
                                  sl(B4, k * TC, TC),
                                  start=(k == 0), stop=(k == NC_I - 1)),
                              waits=W(("s", tok_xb_all), lu(("pAv", 0)),
                                      lu(("pAs", 0))), inc="t")
            tok_s_xd = em("scalar",
                          lambda e: e.activation(xd_bf[:, 0:TC],
                                                 pA[0][0:96, 0:TC], AF.Copy),
                          waits=W(("t", tok_t_xp)), inc="s")
            LU[("pAs", 0)] = ("s", tok_s_xd)
            tok_bg = dma("gpsimd", "dbc", B3[0:1, 0:D_STATE * TC],
                         xd_bf[64:64 + D_STATE, 0:TC],
                         waits=W(("s", tok_s_xd), lu(("wx", 0)), lu(("wx", 1)),
                                 lu(("wz", 0)), lu(("wz", 1))))
            tok_s_bcn = None
            tok_t_bl = []
            for n in range(D_STATE):
                par = n % 2
                tok_t_b = em("tensor",
                             lambda e, n=n, par=par: e.matmul(
                                 pB[par][:, 0:TC], ones_all_bf[0:1, :],
                                 B3[0:1, n * TC:(n + 1) * TC],
                                 start=True, stop=True),
                             waits=W(("dbc", tok_bg), lu("pB%d_s" % par),
                                     lu("pB%d_v" % par), lu("B1free")),
                             inc="t")
                tok_t_bl.append(tok_t_b)
                tok_s_bcn = em("scalar",
                               lambda e, n=n, par=par: e.copy(
                                   sl(B1, n * TC, TC), pB[par][:, 0:TC]),
                               waits=W(("t", tok_t_b)), inc="s")
                LU["pB%d_s" % par] = ("s", tok_s_bcn)
            tok_cg = dma("gpsimd", "dbc", B3[0:1, 0:D_STATE * TC],
                         xd_bf[80:80 + D_STATE, 0:TC],
                         waits=W(("t", tok_t_bl[-1])))
            for n in range(D_STATE):
                par = n % 2
                tok_t_c = em("tensor",
                             lambda e, n=n, par=par: e.matmul(
                                 pB[par][:, 0:TC], ones_all_bf[0:1, :],
                                 B3[0:1, n * TC:(n + 1) * TC],
                                 start=True, stop=True),
                             waits=W(("dbc", tok_cg), ("s", tok_s_bcn)),
                             inc="t")
                tok_s_bcn = em("scalar",
                               lambda e, n=n, par=par: e.copy(
                                   sl(B2, n * TC, TC), pB[par][:, 0:TC]),
                               waits=W(("t", tok_t_c)), inc="s")
                LU["pB%d_s" % par] = ("s", tok_s_bcn)
                LU["bcmm_done"] = ("t", tok_t_c)

            tok_v_ysz = None
            for b in range(NC_I):
                par = b % 2
                tok_t_dt = em("tensor",
                              lambda e, b=b, par=par: e.matmul(
                                  pA[par][:, 0:TC],
                                  wdtT_bf[:, b * 128:(b + 1) * 128],
                                  xd_bf[0:64, 0:TC], start=True, stop=True),
                              waits=W(("s", tok_s_xd), lu(("pAv", par)),
                                      lu(("pAs", par))), inc="t")
                tok_s_sp = em("scalar",
                              lambda e, b=b, par=par: e.activation(
                                  sl(G5, SL_SP1, TC), pA[par][:, 0:TC], AF.Abs,
                                  bias=bdt_c[:, b:b + 1]),
                              waits=W(("t", tok_t_dt), lu("spv")), inc="s")
                tok_s_sp = em("scalar",
                              lambda e: e.activation(
                                  sl(G5, SL_SP2, TC), sl(G5, SL_SP1, TC),
                                  AF.Exp, scale=-1.0), inc="s")
                tok_s_sp = em("scalar",
                              lambda e: e.activation(
                                  sl(G5, SL_SP1, TC), sl(G5, SL_SP2, TC),
                                  AF.Ln, bias=1.0), inc="s")
                tok_v_dt = em("vector",
                              lambda e, b=b, par=par: e.tensor_scalar(
                                  sl(G5, SL_SP2, TC), pA[par][:, 0:TC],
                                  bdt_c[:, b:b + 1], 0.0, ALU.add, ALU.max),
                              waits=W(("s", tok_s_sp), lu(("dtS", par))),
                              inc="v")
                tok_v_dt = em("vector",
                              lambda e, par=par: e.tensor_add(
                                  sl(B7, SL_DT[par], TC), sl(G5, SL_SP2, TC),
                                  sl(G5, SL_SP1, TC)), inc="v")
                LU[("pAv", par)] = ("v", tok_v_dt)
                LU["spv"] = ("v", tok_v_dt)
                tok_v_dtx = em("vector",
                               lambda e, b=b, par=par: e.tensor_mul(
                                   sl(B7, SL_DTX[par], TC), sl(B7, SL_DT[par], TC),
                                   sl(B4, b * TC, TC)), inc="v")
                tok_v_dd = em("vector",
                              lambda e, b=b, par=par: e.tensor_single_scalar(
                                  sl(ddrot, par * 128, 128), eye_bf[:],
                                  dvec_c[:, b:b + 1], ALU.mult),
                              waits=W(lu(("ddT", par))), inc="v")
                tok_t_acc = None
                for n in range(D_STATE):
                    npar = n % 2
                    col = n * NC_I + b
                    prev = (b, n - 2) if n >= 2 else (b - 1, D_STATE - 2 + n)
                    tok_s_da = em("scalar",
                                  lambda e, b=b, n=n, par=par, npar=npar:
                                  e.activation(
                                      sl(G7, SL_DA[npar], TC),
                                      sl(B7, SL_DT[par], TC), AF.Exp,
                                      scale=acol[:, b * D_STATE + n:
                                                 b * D_STATE + n + 1]),
                                  waits=W(lu(("scanv", prev)),
                                          ("v", tok_v_dtx)), inc="s")
                    if n == D_STATE - 1:
                        LU[("dtS", par)] = ("s", tok_s_da)
                    tok_v_sc = em("vector",
                                  lambda e, n=n, par=par, npar=npar: e.tensor_mul(
                                      sl(B7, SL_DBX[npar], TC),
                                      sl(B7, SL_DTX[par], TC),
                                      sl(B1, n * TC, TC)),
                                  waits=W(lu(("acc", prev))), inc="v")
                    tok_v_sc = em("vector",
                                  lambda e, col=col, npar=npar:
                                  e.tensor_tensor_scan(
                                      sl(G7, SL_H[npar], TC),
                                      sl(G7, SL_DA[npar], TC),
                                      sl(B7, SL_DBX[npar], TC),
                                      hcarry[:, 2 * col + 1:2 * col + 2],
                                      ALU.mult, ALU.add),
                                  waits=W(("s", tok_s_da)), inc="v")
                    em("scalar",
                       lambda e, col=col, npar=npar: e.copy(
                           hcarry[:, 2 * col:2 * col + 2],
                           sl(G7, SL_H[npar] + TC - 2, 2)),
                       waits=W(("v", tok_v_sc)), inc="s")
                    if debug and tt == 0 and b == 15 and n == 0:
                        em("vector",
                           lambda e, npar=npar: e.tensor_copy(
                               G1[:, 0:TC], sl(G7, SL_H[npar], TC)), inc="v")
                    tok_v_sc = em("vector",
                                  lambda e, n=n, npar=npar: e.tensor_mul(
                                      sl(B7, SL_PROD[npar], TC),
                                      sl(G7, SL_H[npar], TC),
                                      sl(B2, n * TC, TC)), inc="v")
                    LU[("scanv", (b, n))] = ("v", tok_v_sc)
                    tok_t_acc = em("tensor",
                                   lambda e, b=b, n=n, npar=npar: e.matmul(
                                       pY[b % 2][:, 0:TC], eye_bf[:],
                                       sl(B7, SL_PROD[npar], TC),
                                       start=(n == 0), stop=False),
                                   waits=W(("v", tok_v_sc), lu(("ysz", b % 2))),
                                   inc="t")
                    LU[("acc", (b, n))] = ("t", tok_t_acc)
                tok_t_acc = em("tensor",
                               lambda e, b=b, par=par: e.matmul(
                                   pY[b % 2][:, 0:TC], sl(ddrot, par * 128, 128),
                                   sl(B4, b * TC, TC), start=False, stop=True),
                               waits=W(("v", tok_v_dd)), inc="t")
                LU[("ddT", par)] = ("t", tok_t_acc)
                tok_v_ysz = em("vector",
                               lambda e, b=b: e.tensor_mul(
                                   sl(B5, b * TC, TC), pY[b % 2][:, 0:TC],
                                   sl(B5, b * TC, TC)),
                               waits=W(("t", tok_t_acc)), inc="v")
                LU[("ysz", b % 2)] = ("v", tok_v_ysz)

            tok_v_x1 = None
            for m in range(NC_D):
                par = m % 2
                tok_wo = dma("sync", "dwo%d" % par, sl(B3, SL_WO[par], NC_I * 128),
                             ap(woblk_t, m * NC_I * BLK,
                                [[128, 128], [BLK, NC_I], [1, 128]]),
                             waits=W(lu(("wo", par)), lu("bcmm_done")))
                tok_t_wo = None
                for k in range(NC_I):
                    tok_t_wo = em("tensor",
                                  lambda e, k=k, par=par: e.matmul(
                                      pA[par][:, 0:TC],
                                      sl(B3, SL_WO[par] + k * 128, 128),
                                      sl(B5, k * TC, TC),
                                      start=(k == 0), stop=(k == NC_I - 1)),
                                  waits=W(("dwo%d" % par, tok_wo), ("v", tok_v_ysz),
                                          lu(("pAv", par)), lu(("pAs", par))),
                                  inc="t")
                LU[("wo", par)] = ("t", tok_t_wo)
                tok_v_x1 = em("vector",
                              lambda e, m=m, par=par: e.scalar_tensor_tensor(
                                  sl(xt, m * TC, TC), pA[par][:, 0:TC],
                                  bout_c[:, m:m + 1], sl(xt, m * TC, TC),
                                  ALU.add, ALU.add),
                              waits=W(("t", tok_t_wo)), inc="v")
                LU[("pAv", par)] = ("v", tok_v_x1)
            LU[("x1", tt)] = ("v", tok_v_x1)

        def quant_scales(tok_v_mx, tf, mode):
            nsl = tf // 128
            tok_s_row = None
            tok_v_mc = None
            for j in range(nsl):
                pj = j % 2
                tok_t_tr = em("tensor",
                              lambda e, j=j, pj=pj: e.matmul(
                                  pB[pj][:, 0:128],
                                  sl(G5, SL_FT[0] + j * 128, 128), eye_f[:],
                                  start=True, stop=True),
                              waits=W(("v", tok_v_mx), lu("pB%d_s" % pj),
                                      lu("pB%d_v" % pj)), inc="t")
                tok_v_mc = em("vector",
                              lambda e, j=j, pj=pj: e.reduce_max(
                                  sl(G5, CQ + 8 + j, 1),
                                  pB[pj][:, 0:128], axis=AX.X),
                              waits=W(("t", tok_t_tr)), inc="v")
                LU["pB%d_v" % pj] = ("v", tok_v_mc)
            for j in range(nsl):
                pj = j % 2
                tok_t_r = em("tensor",
                             lambda e, j=j, pj=pj: e.matmul(
                                 pB[pj][0:1, 0:128],
                                 sl(G5, CQ + 8 + j, 1), eye_f[:],
                                 start=True, stop=True),
                             waits=W(("v", tok_v_mc), lu("pB%d_s" % pj),
                                     lu("pB%d_v" % pj)), inc="t")
                tok_s_row = em("scalar",
                               lambda e, j=j, pj=pj: e.copy(
                                   G6[0:1, RB + j * 128:RB + (j + 1) * 128],
                                   pB[pj][0:1, 0:128]),
                               waits=W(("t", tok_t_r)), inc="s")
                LU["pB%d_s" % pj] = ("s", tok_s_row)
            tok_s_bc = None
            for j in range(nsl):
                pj = j % 2
                tok_t_b = em("tensor",
                             lambda e, j=j, pj=pj: e.matmul(
                                 pB[pj][:, 0:128], ones_all_f[0:1, :],
                                 G6[0:1, RB + j * 128:RB + (j + 1) * 128],
                                 start=True, stop=True),
                             waits=W(("s", tok_s_row), lu("pB%d_s" % pj),
                                     lu("pB%d_v" % pj)), inc="t")
                tok_s_bc = em("scalar",
                              lambda e, j=j, pj=pj: e.copy(
                                  sl(G4, BC_P + j * 128, 128),
                                  pB[pj][:, 0:128]),
                              waits=W(("t", tok_t_b)), inc="s")
                LU["pB%d_s" % pj] = ("s", tok_s_bc)
            tok_v_s = em("vector",
                         lambda e, tf=tf: e.tensor_scalar(
                             sl(G4, BC_P, tf), sl(G4, BC_P, tf), 1e-5, None,
                             ALU.max),
                         waits=W(("s", tok_s_bc)), inc="v")
            wis = (0, 1) if mode == 0 else (2,)
            dsts = (BC_Q, BC_X)
            for i, wi in enumerate(wis):
                tok_v_s = em("vector",
                             lambda e, tf=tf, wi=wi, i=i: e.scalar_tensor_tensor(
                                 sl(G4, dsts[i], tf), sl(G4, BC_P, tf),
                                 1.0 / 127.0,
                                 meancol[:, wi:wi + 1].broadcast_to((128, tf)),
                                 ALU.mult, ALU.mult), inc="v")
            tok_v_s = em("vector",
                         lambda e, tf=tf: e.reciprocal(
                             sl(G5, SL_FT[0], tf), sl(G4, BC_P, tf)), inc="v")
            tok_v_s = em("vector",
                         lambda e, tf=tf: e.tensor_scalar(
                             sl(G4, BC_P, tf), sl(G5, SL_FT[0], tf),
                             127.0, None, ALU.mult), inc="v")
            return tok_v_s

        def tile_F(ft, upto=None):
            base = ft * TF
            mtile = base // TC
            xoff = base % TC
            set_barrier()
            xs = lambda c: sl(xt, c * TC + xoff, TF)
            ROF = {"m": 0, "v": 256, "t1": 512, "r": 768}
            tok_s_r, _ = ln_stats(xs, TF, [lu(("x1", mtile))], rowsF, ROF)
            tok_v_be = em("vector", lambda e: e.scalar_tensor_tensor(
                rowsF(0, TF), rowsF(0, TF), -1.0,
                rowsF(768, TF), ALU.mult, ALU.mult),
                waits=W(("s", tok_s_r)), inc="v")
            for i, (roff, boff) in enumerate(((768, BC_AL), (0, BC_BE))):
                tok_s_bc = bcast_row(roff, boff, TF, tok_v_be, i % 2, rowsF)
            tok_v_x2 = None
            for c in range(NC_D):
                tok_v_x2 = em("vector",
                              lambda e, c=c: e.tensor_mul(
                                  sl(G3, c * TF, TF), xs(c), sl(G4, BC_AL, TF)),
                              waits=W(("s", tok_s_bc)), inc="v")
                tok_v_x2 = em("vector",
                              lambda e, c=c: e.tensor_add(
                                  sl(G3, c * TF, TF), sl(G3, c * TF, TF),
                                  sl(G4, BC_BE, TF)), inc="v")
            tok_v_mx = em("vector", lambda e: e.reduce_max(
                sl(G5, SL_FT[0], TF),
                G3[:, 0:NC_D * TF].rearrange("p (c t) -> p t c", t=TF),
                axis=AX.X, apply_absolute_value=True),
                waits=W(("v", tok_v_x2)), inc="v")
            tok_sq = quant_scales(tok_v_mx, TF, 0)
            tok_v_xq = None
            for c in range(NC_D):
                par = c % 2
                tok_v_xq = em("vector",
                              lambda e, c=c, par=par: e.tensor_mul(
                                  sl(G5, SL_FU[par], TF), sl(G3, c * TF, TF),
                                  sl(G4, BC_P, TF)),
                              waits=W(("v", tok_sq)), inc="v")
                tok_v_xq = em("vector",
                              lambda e, par=par: e.tensor_scalar(
                                  wi_rot[:, par * 256:par * 256 + TF].bitcast(F32),
                                  sl(G5, SL_FU[par], TF), MAGIC, None, ALU.add),
                              inc="v")
                tok_v_xq = em("vector",
                              lambda e, c=c, par=par: e.tensor_scalar(
                                  sl(B3, SL_X2Q + c * TF, TF),
                                  wi_rot[:, par * 256:par * 256 + TF].bitcast(F32),
                                  -MAGIC, -0.5, ALU.add, ALU.add), inc="v")
            if upto == "x2q":
                return
            tok_v_gu = None
            for j in range(NC_H):
                par = j % 2
                tok_wg = dma("sync", "dwg%d" % par, sl(B3, SL_WG[par], NC_D * 128),
                             ap(wgblk_t, j * NC_D * BLK,
                                [[128, 128], [BLK, NC_D], [1, 128]]),
                             waits=W(lu(("wgt", par)), lu("blk_fw0"),
                                     lu("blk_fw1")))
                tok_t_g = None
                for k in range(NC_D):
                    tok_t_g = em("tensor",
                                 lambda e, k=k, par=par: e.matmul(
                                     pA[par][:, 0:TF],
                                     sl(B3, SL_WG[par] + k * 128, 128),
                                     sl(B3, SL_X2Q + k * TF, TF),
                                     start=(k == 0), stop=(k == NC_D - 1)),
                                 waits=W(("dwg%d" % par, tok_wg), ("v", tok_v_xq),
                                         lu(("pAv", par)), lu(("pAs", par))),
                                 inc="t")
                LU[("wgt", par)] = ("t", tok_t_g)
                tok_v_go = em("vector",
                              lambda e, j=j, par=par: e.scalar_tensor_tensor(
                                  sl(G5, SL_FT[par], TF), pA[par][:, 0:TF],
                                  corr_g[:, j:j + 1], sl(G4, BC_Q, TF),
                                  ALU.add, ALU.mult),
                              waits=W(("t", tok_t_g), lu("corr"),
                                      lu(("ftS", par))), inc="v")
                LU[("pAv", par)] = ("v", tok_v_go)
                tok_s_gate = em("scalar",
                                lambda e, j=j, par=par: e.activation(
                                    sl(G1, j * TF, TF), sl(G5, SL_FT[par], TF),
                                    AF.Silu),
                                waits=W(("v", tok_v_go)), inc="s")
                LU[("ftS", par)] = ("s", tok_s_gate)
                tok_wu = dma("sync", "dwg%d" % par, sl(B3, SL_WG[par], NC_D * 128),
                             ap(wublk_t, j * NC_D * BLK,
                                [[128, 128], [BLK, NC_D], [1, 128]]),
                             waits=W(("t", tok_t_g)))
                tok_t_u = None
                for k in range(NC_D):
                    tok_t_u = em("tensor",
                                 lambda e, k=k, par=par: e.matmul(
                                     pY[par][:, 0:TF],
                                     sl(B3, SL_WG[par] + k * 128, 128),
                                     sl(B3, SL_X2Q + k * TF, TF),
                                     start=(k == 0), stop=(k == NC_D - 1)),
                                 waits=W(("dwg%d" % par, tok_wu),
                                         lu(("pYv", par))),
                                 inc="t")
                LU[("wgt", par)] = ("t", tok_t_u)
                tok_v_uo = em("vector",
                              lambda e, j=j, par=par: e.scalar_tensor_tensor(
                                  sl(G5, SL_FU[par], TF), pY[par][:, 0:TF],
                                  corr_u[:, j:j + 1], sl(G4, BC_X, TF),
                                  ALU.add, ALU.mult),
                              waits=W(("t", tok_t_u)), inc="v")
                LU[("pYv", par)] = ("v", tok_v_uo)
                tok_v_gu = em("vector",
                              lambda e, j=j, par=par: e.tensor_mul(
                                  sl(G1, j * TF, TF), sl(G5, SL_FU[par], TF),
                                  sl(G1, j * TF, TF)),
                              waits=W(("s", tok_s_gate)), inc="v")
            tok_v_mx = em("vector", lambda e: e.reduce_max(
                sl(G5, SL_FT[0], TF),
                G1[:, 0:NC_H * TF].rearrange("p (c t) -> p t c", t=TF),
                axis=AX.X, apply_absolute_value=True),
                waits=W(("v", tok_v_gu)), inc="v")
            tok_sq3 = quant_scales(tok_v_mx, TF, 1)
            tok_v_q3 = None
            for c in range(NC_H):
                par = c % 2
                tok_v_q3 = em("vector",
                              lambda e, c=c, par=par: e.tensor_mul(
                                  sl(G5, SL_FU[par], TF), sl(G1, c * TF, TF),
                                  sl(G4, BC_P, TF)),
                              waits=W(("v", tok_sq3), lu("B1free")), inc="v")
                tok_v_q3 = em("vector",
                              lambda e, par=par: e.tensor_scalar(
                                  wi_rot[:, par * 256:par * 256 + TF].bitcast(F32),
                                  sl(G5, SL_FU[par], TF), MAGIC, None, ALU.add),
                              inc="v")
                tok_v_q3 = em("vector",
                              lambda e, c=c, par=par: e.tensor_scalar(
                                  sl(B1, c * TF, TF),
                                  wi_rot[:, par * 256:par * 256 + TF].bitcast(F32),
                                  -MAGIC, -0.5, ALU.add, ALU.add), inc="v")
            tok_t_d = None
            for m in range(NC_D):
                par = m % 2
                tok_wd = dma("sync", "dwd%d" % par, sl(B4, SL_WD2[par], NC_H * 128),
                             ap(wdblk_t, m * NC_H * BLK,
                                [[128, 128], [BLK, NC_H], [1, 128]]),
                             waits=W(lu(("wdt", par))))
                tok_t_d = None
                for k in range(NC_H):
                    tok_t_d = em("tensor",
                                 lambda e, k=k, par=par: e.matmul(
                                     pA[par][:, 0:TF],
                                     sl(B4, SL_WD2[par] + k * 128, 128),
                                     sl(B1, k * TF, TF),
                                     start=(k == 0), stop=(k == NC_H - 1)),
                                 waits=W(("dwd%d" % par, tok_wd), ("v", tok_v_q3),
                                         lu(("pAv", par)), lu(("pAs", par))),
                                 inc="t")
                LU[("wdt", par)] = ("t", tok_t_d)
                tok_v_d = em("vector",
                             lambda e, m=m, par=par: e.scalar_tensor_tensor(
                                 sl(G5, SL_OB[par], TF), pA[par][:, 0:TF],
                                 corr_d[:, m:m + 1], sl(G4, BC_Q, TF),
                                 ALU.add, ALU.mult),
                             waits=W(("t", tok_t_d), lu(("outd", par))),
                             inc="v")
                LU[("pAv", par)] = ("v", tok_v_d)
                tok_v_d2 = em("vector",
                              lambda e, m=m, par=par: e.tensor_add(
                                  sl(G5, SL_OB[par], TF), sl(G5, SL_OB[par], TF),
                                  xs(m)), inc="v")
                tok_o = dma("gpsimd", "dou%d" % par,
                            ap(out_t, m * 128 * L + base, [[L, 128], [1, TF]]),
                            sl(G5, SL_OB[par], TF),
                            waits=[("v", tok_v_d2)])
                LU[("outd", par)] = ("dou%d" % par, tok_o)
            LU["B4free"] = ("t", tok_t_d)
            LU["B1free"] = ("t", tok_t_d)

        if debug == 3:
            d1 = nc.dram_tensor("dbg_absq", [128, 48], F32, kind="ExternalOutput")
            d2 = nc.dram_tensor("dbg_rows", [1, 80], F32, kind="ExternalOutput")
            d3 = nc.dram_tensor("dbg_mean", [128, 3], F32, kind="ExternalOutput")
            d4 = nc.dram_tensor("dbg_th", [128, 6], F32, kind="ExternalOutput")
            set_barrier()
            dma("gpsimd", "dg", bass.AP(d1, 0, [[48, 128], [1, 48]]), G7[:, 0:48], waits=W())
            em("scalar", lambda e: e.copy(G6[0:1, 16:64], pB[0][0:1, 0:48]),
               waits=W(), inc="s")
            set_barrier()
            dma("gpsimd", "dg", bass.AP(d2, 0, [[80, 1], [1, 80]]), G6[0:1, 0:80], waits=W())
            dma("gpsimd", "dg", bass.AP(d3, 0, [[3, 128], [1, 3]]), meancol[:, 0:3], waits=W())
            dma("gpsimd", "dg", bass.AP(d4, 0, [[6, 128], [1, 6]]), thcol[:, 0:6], waits=W())
        elif debug and debug != 5:
            if debug in (2, 4):
                dbg_specs = [
                    ("dbg_x2q", B3, BF16, 2048), ("dbg_gu", G1, F32, 8192),
                    ("dbg_guq", B1, BF16, 8192), ("dbg_g4", G4, F32, 2048),
                    ("dbg_corrg", corr_g, F32, 32), ("dbg_corrd", corr_d, F32, 8),
                    ("dbg_mean", meancol, F32, 3), ("dbg_x1", xt, F32, 4096),
                    ("dbg_cq", G5, F32, 3328), ("dbg_g6r", G6, F32, 2064),
                ]
            else:
                dbg_specs = [
                    ("dbg_xn", B6, BF16, 4096), ("dbg_xb", B4, BF16, 8192),
                    ("dbg_ysz", B5, BF16, 8192), ("dbg_x1", xt, F32, 4096),
                    ("dbg_hc", hcarry, F32, 512), ("dbg_B", B1, BF16, 8192),
                    ("dbg_C", B2, BF16, 8192), ("dbg_g4", G4, F32, 2048),
                    ("dbg_b7", B7, BF16, 4096), ("dbg_g7", G7, F32, 2048),
                    ("dbg_g1", G1, F32, 8192),
                ]
            dbg_ts = {}
            for nm, buf, dt_, n in dbg_specs:
                dbg_ts[nm] = nc.dram_tensor(nm, [128, n], dt_, kind="ExternalOutput")
            dbg_xd_t = nc.dram_tensor("dbg_xd", [96, 512], BF16, kind="ExternalOutput")
            tile_M(0)
            if debug == 1:
                set_barrier()
                em("vector", lambda e: e.tensor_copy(
                    G1[:, 512:1024], hcarry[:, 0:512]), waits=W(), inc="v")
            if debug == 2:
                tile_F(0)
            elif debug == 4:
                tile_F(0, upto="x2q")
            set_barrier()
            for nm, buf, dt_, n in dbg_specs:
                dma("gpsimd", "dg", bass.AP(dbg_ts[nm], 0, [[n, 128], [1, n]]),
                    buf[:, 0:n], waits=W())
            dma("gpsimd", "dg", bass.AP(dbg_xd_t, 0, [[512, 96], [1, 512]]),
                xd_bf[0:96, 0:512], waits=W())
        else:
            if debug == 5:
                x1d_ts = [nc.dram_tensor("dbg_x1t%d" % tt, [128, NC_D * TC],
                                         F32, kind="ExternalOutput")
                          for tt in range(NT)]
            for tt in range(NT):
                tile_M(tt)
                if debug == 5:
                    set_barrier()
                    dma("gpsimd", "dg",
                        bass.AP(x1d_ts[tt], 0,
                                [[NC_D * TC, 128], [1, NC_D * TC]]),
                        xt[:, 0:NC_D * TC], waits=W())
                for sub in range(TC // TF):
                    tile_F(tt * (TC // TF) + sub)

        em("gpsimd", lambda e: None,
           waits=[(s_, sch.cnt.get(s_, 0)) for s_ in
                  ("dou0", "dou1", "dg", "dgm", "dgw0", "dgw1", "dbc", "drb")
                  if sch.cnt.get(s_, 0)])

        with nc.Block() as block:
            sch.realize(block, SEMS)

    return nc


_NC_CACHE = [None]


def kernel(**inputs):
    x = np.asarray(inputs["L0_attn_x"])
    t = lambda a: np.ascontiguousarray(np.asarray(a).T)
    base = {
        "wxT": t(inputs["L0_attn_Wx"]),
        "wzT": t(inputs["L0_attn_Wz"]),
        "wxpT": t(inputs["L0_attn_Wxp"]),
        "wdtT": t(inputs["L0_attn_Wdt"]),
        "woutT": t(inputs["L0_attn_Wout"]),
        "wgT": t(inputs["L0_ffn_w_gate"]),
        "wuT": t(inputs["L0_ffn_w_up"]),
        "wdT": t(inputs["L0_ffn_w_down"]),
        "normw": np.ascontiguousarray(inputs["L0_attn_norm_w"]),
        "bx": np.ascontiguousarray(inputs["L0_attn_bx"]),
        "bz": np.ascontiguousarray(inputs["L0_attn_bz"]),
        "convw": np.ascontiguousarray(
            np.asarray(inputs["L0_attn_conv_w"]).reshape(D_INNER, 4)),
        "convb": np.ascontiguousarray(inputs["L0_attn_conv_b"]),
        "bdt": np.ascontiguousarray(inputs["L0_attn_bdt"]),
        "alog": np.ascontiguousarray(inputs["L0_attn_A_log"]),
        "dvec": np.ascontiguousarray(inputs["L0_attn_D"]),
        "bout": np.ascontiguousarray(inputs["L0_attn_bout"]),
        "eyef": np.eye(128, dtype=np.float32),
    }
    in_maps = [dict(base, x=t(x[b])) for b in range(8)]
    if _NC_CACHE[0] is None:
        _NC_CACHE[0] = build_nc()
    res = run_bass_kernel_spmd(_NC_CACHE[0], in_maps, core_ids=list(range(8)))
    return np.ascontiguousarray(
        np.stack([res.results[b]["out"].T for b in range(8)])).astype(
            np.float32, copy=False)
